# version 06
# baseline (speedup 1.0000x reference)
import numpy as np
from dataclasses import dataclass
from contextlib import ExitStack

import concourse.bass as bass
import concourse.mybir as mybir
from concourse.bass_utils import run_bass_kernel_spmd
from concourse.aot_env import allow_ici_all_to_all

F32 = mybir.dt.float32
BF16 = mybir.dt.bfloat16
AF = mybir.ActivationFunctionType
ALU = mybir.AluOpType

NC = 8
D = 2048
DC = 16
NST = 16
RK = 64
NE = 8
HID = 1408
MJ = 11
HS = 2816
MJS = 22
CH = 256
HB = 2
EPS = 1e-5


@dataclass
class P:
    TPC: int = 1024
    dump: bool = False
    interleave: bool = True
    phase: str = "all"
    cut: int = 99

    @property
    def S(self):
        return NC * self.TPC

    @property
    def TT(self):
        return min(512, self.TPC)

    @property
    def NTT(self):
        return max(1, self.TPC // self.TT)


class Wk:
    ENGS = ("sync", "scalar", "vector", "tensor", "gpsimd")

    def __init__(self, nc, ctx):
        self.nc = nc
        self.ctx = ctx
        self.plan = {e: [] for e in self.ENGS}
        self.sems = {}
        self.count = {}
        self.labels = {}
        for name in ("v", "s", "t", "g", "cc"):
            self._mksem(name)

    def _mksem(self, name):
        if name not in self.sems:
            self.sems[name] = self.ctx.enter_context(self.nc.semaphore(name))
            self.count[name] = 0
        return self.sems[name]

    def emit(self, eng, fn, inc=None, amt=1, label=None):
        if inc is not None:
            self.count[inc] += amt
            if label is not None:
                self.labels[label] = (inc, self.count[inc])
            sem = self.sems[inc]

            def wrapped(e, fn=fn, sem=sem, amt=amt):
                fn(e).then_inc(sem, amt)
            self.plan[eng].append(wrapped)
        else:
            assert label is None
            self.plan[eng].append(fn)

    def alias(self, new, old):
        self.labels[new] = self.labels[old]

    def alias_late(self, new, old):
        self._late = getattr(self, "_late", [])
        self._late.append((new, old))

    def resolve_late(self):
        for new, old in getattr(self, "_late", []):
            self.labels[new] = self.labels[old]

    def wait(self, eng, label):
        def wrapped(e, label=label):
            name, val = self.labels[label]
            e.wait_ge(self.sems[name], val)
        self.plan[eng].append(wrapped)

    def dma(self, queue_eng, out, in_, label=None, sem="q"):
        self._mksem(sem)
        if sem == "qdmp" and self.count[sem] > 0:
            val = self.count[sem]

            def wr(e, val=val):
                e.wait_ge(self.sems["qdmp"], val)
            self.plan[queue_eng].append(wr)
        self.emit(queue_eng, lambda e, o=out, i=in_: e.dma_start(out=o, in_=i),
                  inc=sem, amt=16, label=label)


def build(p: P):
    nc = bass.Bass("TRN2", target_bir_lowering=False,
                   detect_race_conditions=False)
    TPC, S, TT, NTT = p.TPC, p.S, p.TT, p.NTT
    TSUB = TPC // 128
    NSP = (TPC + 511) // 512
    ctx = ExitStack()
    nc._build_ctx = ctx

    def din(name, shape):
        return nc.dram_tensor(name, shape, F32, kind="ExternalInput").ap()

    xT = din("xT", [D, S])
    x_own_tok = din("x_own_tok", [D, TPC])
    x_own_ch = din("x_own_ch", [CH, S])
    alog_own = din("alog_own", [128, HB * NST])
    d_own = din("d_own", [128, HB])
    bdt_own = din("bdt_own", [128, HB])
    wdt2T_own = din("wdt2T_own", [RK, CH])
    wprojT = din("wprojT", [D, RK + 2 * NST])
    routerT = din("routerT", [D, NE])
    wg_d = din("wg", [NE, MJ, 128, DC * 128])
    wu_d = din("wu", [NE, MJ, 128, DC * 128])
    wd_d = din("wd", [NE, DC, 128, MJ * 128])
    sg_d = din("sg", [MJS, 128, DC * 128])
    su_d = din("su", [MJS, 128, DC * 128])
    sd_d = din("sd", [DC, 128, MJS * 128])
    eye = din("eye128", [128, 128])

    out_t = nc.dram_tensor("out", [D, TPC], F32, kind="ExternalOutput").ap()
    dumps = {}
    if p.dump:
        for nm, sh, ddt in [("dmp_stats", [2, TPC], F32),
                            ("dmp_dtr", [RK, S], F32),
                            ("dmp_bc", [2 * NST, S], F32),
                            ("dmp_u", [CH, S], F32),
                            ("dmp_dt", [CH, S], BF16),
                            ("dmp_ymix", [CH, S], BF16),
                            ("dmp_gates", [NE, TPC], F32),
                            ("dmp_xn", [D, TPC], F32)]:
            dumps[nm] = nc.dram_tensor(nm, sh, ddt, kind="ExternalOutput").ap()

    stats_mine = nc.dram_tensor("stats_mine", [2, TPC], F32).ap()
    stats_all = nc.dram_tensor("stats_all", [NC, 2, TPC], F32).ap()
    bounce_bc = nc.dram_tensor("bounce_bc", [2 * NST, S], BF16).ap()
    bounce_g = nc.dram_tensor("bounce_g", [NE, TPC], F32).ap()
    bounce_u = nc.dram_tensor("bounce_u", [CH, S], BF16).ap()
    bounce_dtr = nc.dram_tensor("bounce_dtr", [RK, S], BF16).ap()
    bounce_dt = nc.dram_tensor("bounce_dt", [CH, S], BF16).ap()
    aa_in = nc.dram_tensor("aa_in", [NC, CH, TPC], BF16).ap()
    aa_out = nc.dram_tensor("aa_out", [NC, CH, TPC], BF16).ap()

    sb = {}

    def t(name, shape, dt=F32):
        sb[name] = ctx.enter_context(nc.sbuf_tensor(name, shape, dt))
        return sb[name]

    xn = t("xn", [128, DC * TPC], BF16)
    y_tot = t("y_tot", [128, DC * TPC], BF16)
    dtrp = t("dtrp", [RK, TPC], BF16)
    dt_sc = t("dt_sc", [128, 2 * TPC], BF16)
    t("dtrw", [RK, 2 * TPC], BF16)
    cstage = t("cstage", [128, 2048], F32)
    wprojb = t("wprojb", [128, DC * 96], BF16)
    routb = t("routb", [128, DC * NE], BF16)
    wdt2b = t("wdt2b", [RK, CH], BF16)
    aneg = t("aneg", [128, HB * NST], F32)
    dcol = t("dcol", [128, HB], F32)
    bcol = t("bcol", [128, HB], F32)
    wsum_neg = t("wsum_neg", [96, 1], F32)
    eye_sb = t("eye_sb", [128, 128], F32)
    ones_b = t("ones_b", [128, 1], BF16)
    ones_f = t("ones_f", [128, 1], F32)
    zcol = t("zcol", [128, 1], F32)
    carry = t("carry", [128, HB * NST], F32)
    ccst = t("ccst", [128, 4], F32)

    gtmp = t("gtmp", [128, 6 * NE], F32)
    gates_fm = t("gates_fm", [NE, TPC], F32)

    xo = t("xo", [128, 2 * TPC], F32)
    xpieceb = t("xpieceb", [128, 4 * TPC], BF16)
    ar1 = t("ar1", [128, 4 * TPC], F32)
    mb = ar1[:, 0:TPC]
    rsb = ar1[:, TPC:2 * TPC]
    sq = ar1[:, 2 * TPC:3 * TPC]
    fx = ar1[0:96, 3 * TPC:4 * TPC]
    bc_piece = t("bc_piece", [2 * NST, TPC], BF16)

    a_t = cstage if 2 * TPC <= 2048 else t("a_t", [128, 2 * TPC], F32)
    bu_t = t("bu_t", [128, TPC], F32)
    yt_t = t("yt_t", [128, TPC], BF16)
    mrow = bu_t[0:1, 0:TPC]
    rsrow = cstage[0:1, TPC:2 * TPC]
    srow = cstage[0:1, 0:TPC]
    st_t = t("st_t", [128, TPC], BF16)
    Bb = t("Bb", [128, 2 * TPC], BF16)
    Cb = t("Cb", [128, 2 * TPC], BF16)
    spl = t("spl", [128, TPC], BF16)
    dtu_w = t("dtu_w", [128, HB * TPC], BF16)
    uw = t("uw", [128, 2 * TPC], BF16)
    y_w = t("y_w", [128, HB * TPC], BF16)
    dtp = y_w

    wbf = t("wbf", [128, 4 * 2048], BF16)
    hprime = t("hprime", [128, MJ * 2 * TT], BF16)
    s_t = t("s_t", [128, 2 * TT], BF16)
    q_t = t("q_t", [128, 2 * TT], BF16)
    g_bc = t("g_bc", [128, TPC], BF16)

    ymr = xpieceb
    xfin = xo
    ofin = ar1[:, 0:2 * TPC]

    ps = [ctx.enter_context(nc.psum_tensor(f"ps{i}", [128, 512], F32))
          for i in range(8)]

    w = Wk(nc, ctx)

    def act(out, in_, func, bias=0.0, scale=1.0, label=None):
        def fn(e, out=out, in_=in_, func=func, bias=bias, scale=scale):
            return e.activation(out=out, in_=in_, func=func, bias=bias,
                                scale=scale)
        w.emit("scalar", fn, inc="s" if label else None, label=label)

    _vs = [0]

    def vop(name, *args, eng="vector", label=None, **kw):
        def fn(e, name=name, args=args, kw=kw):
            return getattr(e, name)(*args, **kw)
        w.emit(eng, fn, inc=("v" if eng == "vector" else "g") if label else None,
               label=label)

    def vsync(name, *args, **kw):
        _vs[0] += 1
        lbl = f"_vs{_vs[0]}"
        vop(name, *args, label=lbl, **kw)
        w.wait("vector", lbl)

    def mm(out, lhsT, rhs, start, stop, label=None):
        def fn(e, out=out, lhsT=lhsT, rhs=rhs, start=start, stop=stop):
            return e.matmul(out, lhsT, rhs, start=start, stop=stop)
        w.emit("tensor", fn, inc="t" if label else None, label=label)

    def chunk(tile, i, width, rows=128):
        return tile[0:rows, i * width:(i + 1) * width]

    w.dma("sync", cstage[:, 0:DC * 96],
          wprojT.rearrange("(c p) k -> p c k", p=128), sem="qc")
    w.dma("sync", cstage[:, DC * 96:DC * 96 + DC * NE],
          routerT.rearrange("(c p) k -> p c k", p=128), sem="qc")
    w.dma("sync", cstage[0:128, 1664:1664 + HB * NST], alog_own, sem="qc")
    w.dma("sync", cstage[0:RK, 1696:1696 + CH], wdt2T_own, sem="qc")
    w.dma("sync", eye_sb[:], eye, sem="qc")
    w.dma("sync", dcol[:], d_own, sem="qc")
    w.dma("sync", bcol[:], bdt_own, label="ld_const", sem="qc")

    w.wait("vector", "ld_const")
    vop("memset", ccst[:, 0:1], EPS)
    vop("memset", ccst[:, 1:2], -0.5)
    vop("memset", ccst[:, 2:3], 0.5)
    vop("memset", ones_b[:], 1.0)
    vop("memset", ones_f[:], 1.0)
    vop("memset", zcol[:], 0.0)
    vop("tensor_copy", out=wprojb[:], in_=cstage[:, 0:DC * 96])
    vop("tensor_copy", out=routb[:], in_=cstage[:, DC * 96:DC * 96 + DC * NE])
    vop("tensor_copy", out=wdt2b[:], in_=cstage[0:RK, 1696:1696 + CH])
    for i in range(DC):
        vop("memset", chunk(y_tot, i, TPC), 0.0)
    vop("engine_nop", label="consts_cast")
    w.wait("scalar", "ld_const")
    act(aneg[:], cstage[0:128, 1664:1664 + HB * NST], AF.Exp, label="an1")
    w.wait("vector", "an1")
    vop("tensor_scalar_mul", aneg[:], aneg[:], -1.0, label="aneg_done")

    w.wait("tensor", "consts_cast")
    for c in range(DC):
        mm(ps[6][0:96, 0:1], chunk(wprojb, c, 96), ones_b[:],
           start=(c == 0), stop=(c == DC - 1),
           label="wsum_mm" if c == DC - 1 else None)
    w.wait("vector", "wsum_mm")
    vop("tensor_scalar_mul", wsum_neg[:], ps[6][0:96, 0:1], -1.0,
        label="wsum_done")

    for i in range(DC) if p.cut >= 1 else []:
        if i >= 2:
            w.wait("scalar", f"stat_mm_{i-2}")
        w.dma("scalar", chunk(xo, i % 2, TPC),
              x_own_tok[i * 128:(i + 1) * 128, :], label=f"xo_{i}",
              sem=f"qxo{i % 2}")
        w.wait("scalar", f"xo_{i}")
        if i >= 1:
            w.wait("scalar", f"stat_mm_{i-1}")
        act(sq, chunk(xo, i % 2, TPC), AF.Square, label=f"sqr_{i}")
        if i == 0:
            w.wait("tensor", "wsum_done")
        w.wait("tensor", f"sqr_{i}")
        for h in range(NSP):
            cw = min(512, TPC - h * 512)
            mm(ps[4 + h][0:1, 0:cw], ones_f[:],
               chunk(xo, i % 2, TPC)[:, h * 512:h * 512 + cw],
               start=(i == 0), stop=(i == DC - 1))
            mm(ps[6 + h][0:1, 0:cw], ones_f[:], sq[:, h * 512:h * 512 + cw],
               start=(i == 0), stop=(i == DC - 1),
               label=f"stat_mm_{i}" if h == NSP - 1 else None)
    if p.cut >= 1:
      w.wait("vector", f"stat_mm_{DC-1}")
    for h in range(NSP):
        cw = min(512, TPC - h * 512)
        vop("tensor_scalar_mul", mrow[:, h * 512:h * 512 + cw],
            ps[4 + h][0:1, 0:cw], 1.0 / D)
        vop("tensor_scalar_mul", srow[:, h * 512:h * 512 + cw],
            ps[6 + h][0:1, 0:cw], 1.0 / D,
            label="stats_evac" if h == NSP - 1 else None)
    vop("tensor_mul", sq[0:1, 0:TPC], mrow, mrow)
    vop("tensor_sub", srow, srow, sq[0:1, 0:TPC],
        label="var_done")
    w.wait("scalar", "var_done")
    act(srow, srow, AF.Ln, bias=ccst[0:1, 0:1],
        label="lnv")
    act(rsrow, srow, AF.Exp, scale=ccst[0:1, 1:2],
        label="rs_done")
    w.wait("scalar", "rs_done")
    w.dma("scalar", stats_mine[0:1, :], mrow, sem="qst")
    w.dma("scalar", stats_mine[1:2, :], rsrow, label="stats_out", sem="qst")
    if p.dump:
        w.dma("scalar", dumps["dmp_stats"][0:1, :], mrow, sem="qdmp")
        w.dma("scalar", dumps["dmp_stats"][1:2, :], rsrow, label="dstat",
              sem="qdmp")

    def coll(kind, op, ins, outs, label):
        def fn(e, kind=kind, op=op, ins=ins, outs=outs):
            if kind == "AllToAll":
                with allow_ici_all_to_all():
                    return e.collective_compute(
                        kind, op, replica_groups=[list(range(NC))],
                        ins=ins, outs=outs)
            return e.collective_compute(
                kind, op, replica_groups=[list(range(NC))], ins=ins, outs=outs)
        w.emit("gpsimd", fn, inc="cc", label=label)

    if p.cut >= 2:
        w.wait("gpsimd", "stats_out")
        coll("AllGather", ALU.bypass, [stats_mine.opt()], [stats_all.opt()],
             "cc_stats")

    for i in range(DC) if p.cut >= 3 else []:
        if i >= 2:
            w.wait("scalar", f"xn_{i-2}")
        w.dma("scalar", chunk(xo, i % 2, TPC),
              x_own_tok[i * 128:(i + 1) * 128, :], label=f"xo2_{i}",
              sem=f"qxo{i % 2}")
        if i == 0:
            w.wait("scalar", "rs_done")
            w.dma("scalar", mb,
                  stats_mine[0:1, :].broadcast_to([128, TPC]), sem="qmr")
            w.dma("scalar", rsb,
                  stats_mine[1:2, :].broadcast_to([128, TPC]), label="ownbc",
                  sem="qmr")
        w.wait("vector", f"xo2_{i}")
        if i == 0:
            w.wait("vector", "ownbc")
        vop("tensor_sub", sq, chunk(xo, i % 2, TPC), mb)
        vop("tensor_mul", chunk(xn, i, TPC), sq, rsb,
            label=f"xn_{i}")

    w.wait("tensor", f"xn_{DC-1}") if p.cut >= 31 else None
    w.wait("tensor", "var_done") if p.cut >= 31 else None
    for ts in range(TSUB) if p.cut >= 31 else []:
        if ts >= 1:
            w.wait("tensor", f"rlec_{ts-1}")
        for c in range(DC):
            mm(ps[6][:, ts * NE:(ts + 1) * NE],
               chunk(xn, c, TPC)[:, ts * 128:(ts + 1) * 128],
               chunk(routb, c, NE), start=(c == 0), stop=(c == DC - 1),
               label=f"rl_{ts}" if c == DC - 1 else None)
        w.wait("vector", f"rl_{ts}")
        L = gtmp[:, 0:NE]
        vop("tensor_copy", out=L, in_=ps[6][:, ts * NE:(ts + 1) * NE],
            label=f"rlec_{ts}")
        w.wait("vector", f"rlec_{ts}")
        if p.cut == 31:
            w.alias(f"gw_{ts}", f"rlec_{ts}")
            continue
        mx = gtmp[:, NE:NE + 1]
        vsync("reduce_max", mx, L, axis=mybir.AxisListType.X)
        vop("tensor_scalar_mul", mx, mx, -1.0, label=f"gmx_{ts}")
        w.wait("vector", f"gmx_{ts}")
        w.wait("scalar", f"gmx_{ts}")
        act(L, L, AF.Exp, bias=mx, label=f"gex_{ts}")
        w.wait("vector", f"gex_{ts}")
        m1 = gtmp[:, NE + 1:NE + 2]
        vsync("reduce_max", m1, L, axis=mybir.AxisListType.X)
        msk = gtmp[:, 2 * NE:3 * NE]
        vsync("tensor_scalar", msk, L, m1, None, ALU.is_ge)
        vsync("tensor_scalar", msk, msk, -1.0, 1.0, ALU.mult, ALU.add)
        g2 = gtmp[:, 3 * NE:4 * NE]
        vsync("tensor_mul", g2, L, msk)
        m2 = gtmp[:, NE + 2:NE + 3]
        vsync("reduce_max", m2, g2, axis=mybir.AxisListType.X)
        vsync("tensor_scalar", msk, L, m2, None, ALU.is_ge)
        den = gtmp[:, NE + 3:NE + 4]
        vsync("tensor_add", den, m1, m2)
        rec = gtmp[:, NE + 4:NE + 5]
        vsync("reciprocal", rec, den)
        vsync("tensor_mul", L, L, msk)
        wslot = gtmp[:, (4 + (ts % 2)) * NE:(5 + (ts % 2)) * NE]
        vop("tensor_scalar_mul", wslot, L, rec, label=f"gw_{ts}")
        w.wait("vector", f"gw_{ts}")
        if p.cut == 32:
            continue
        w.wait("tensor", f"gw_{ts}")
        if ts >= 1:
            w.wait("tensor", f"gfm_{ts-1}")

        def tp(e, wslot=wslot):
            return e.transpose(ps[7][0:NE, 0:128], wslot, eye_sb[:])
        w.emit("tensor", tp, inc="t", label=f"gt_{ts}")
        w.wait("vector", f"gt_{ts}")
        vop("tensor_copy", out=gates_fm[:, ts * 128:(ts + 1) * 128],
            in_=ps[7][0:NE, 0:128], label=f"gfm_{ts}")
    if p.cut >= 34:
        w.wait("scalar", f"gfm_{TSUB-1}")
        w.dma("scalar", bounce_g[:, :], gates_fm[:, 0:TPC], label="g_out",
              sem="qgo")
    if p.dump and p.cut >= 34:
        w.dma("scalar", dumps["dmp_gates"], gates_fm[:, 0:TPC], sem="qdmp")
        for i in range(DC):
            w.wait("vector", f"xn_{i}")
            vop("tensor_copy", out=sq, in_=chunk(xn, i, TPC),
                label=f"dxn_{i}")
            w.wait("scalar", f"dxn_{i}")
            w.dma("scalar", dumps["dmp_xn"][i * 128:(i + 1) * 128, :],
                  sq, label=f"dxnd_{i}", sem="qdmp")
            w.wait("vector", f"dxnd_{i}")

    def emit_p34():
      for g in range(NC) if p.cut >= 40 else []:
          w.wait("scalar", "cc_stats")
          if g == 0:
              w.wait("scalar", f"xn_{DC-1}")
          else:
              w.wait("scalar", f"fix_{g-1}")
          w.dma("scalar", mb,
                stats_all[g, 0:1, :].broadcast_to([128, TPC]), sem="qmr")
          w.dma("scalar", rsb,
                stats_all[g, 1:2, :].broadcast_to([128, TPC]), label=f"gbc_{g}",
                sem="qmr")
          for hb in range(HB):
              idx = g * HB + hb
              if idx >= 2:
                  w.wait("scalar", f"u_{idx-2}")
              elif g == 0:
                  w.wait("scalar", f"xn_{DC-1}")
              w.dma("scalar", chunk(xo, idx % 2, TPC),
                    x_own_ch[hb * 128:(hb + 1) * 128, g * TPC:(g + 1) * TPC],
                    label=f"xch_{idx}", sem=f"qxo{idx % 2}")
              w.wait("vector", f"xch_{idx}")
              if hb == 0:
                  w.wait("vector", f"gbc_{g}")
              if idx >= 2:
                  w.wait("vector", f"ubo_{idx-2}")
              vop("tensor_sub", sq, chunk(xo, idx % 2, TPC),
                  mb)
              vop("tensor_mul", chunk(uw, idx % 2, TPC), sq,
                  rsb, label=f"u_{idx}")
              w.wait("scalar", f"u_{idx}")
              w.dma("scalar",
                    bounce_u[hb * 128:(hb + 1) * 128, g * TPC:(g + 1) * TPC],
                    chunk(uw, idx % 2, TPC), label=f"ubo_{idx}",
                    sem=f"qub{idx % 2}")
              if p.dump:
                  vop("tensor_copy", out=sq,
                      in_=chunk(uw, idx % 2, TPC), label=f"du_{idx}")
                  w.wait("scalar", f"du_{idx}")
                  w.dma("scalar",
                        dumps["dmp_u"][hb * 128:(hb + 1) * 128,
                                       g * TPC:(g + 1) * TPC],
                        sq, label=f"dud_{idx}", sem="qdmp")
                  w.wait("vector", f"dud_{idx}")
          for c in range(DC):
              idx = g * DC + c
              if idx >= 4:
                  w.wait("gpsimd", f"pmm_{idx-4}")
              w.dma("gpsimd", chunk(xpieceb, idx % 4, TPC),
                    xT[c * 128:(c + 1) * 128, g * TPC:(g + 1) * TPC],
                    label=f"xp_{idx}", sem=f"qxp{idx % 4}")
              w.wait("tensor", f"xp_{idx}")
              if c == 0:
                  w.wait("tensor", "wsum_done")
                  w.wait("tensor", f"gfm_{TSUB-1}")
                  if g >= 1:
                      w.wait("tensor", f"fix_{g-1}")
              for h in range(NSP):
                  cw = min(512, TPC - h * 512)
                  mm(ps[6 + h][0:96, 0:cw], chunk(wprojb, c, 96),
                     chunk(xpieceb, idx % 4, TPC)[:, h * 512:h * 512 + cw],
                     start=(c == 0), stop=(c == DC - 1),
                     label=f"pmm_{idx}" if h == NSP - 1 else None)
              if c == DC - 1:
                  w.alias(f"prj_{g}", f"pmm_{idx}")
          w.wait("vector", f"prj_{g}")
          for h in range(NSP):
              cw = min(512, TPC - h * 512)
              cs = slice(h * 512, h * 512 + cw)
              vop("scalar_tensor_tensor", fx[0:96, cs], mb[0:96, cs],
                  wsum_neg[0:96, 0:1], ps[6 + h][0:96, 0:cw], ALU.mult, ALU.add)
          vop("tensor_mul", fx[0:96, 0:TPC], fx[0:96, 0:TPC], rsb[0:96, :])
          if g >= 1:
              w.wait("vector", f"dtrb_{g-1}")
              w.wait("vector", f"bcb_{g-1}")
          vop("tensor_copy", out=dtrp[:, 0:TPC], in_=fx[0:RK, 0:TPC])
          vop("tensor_copy", out=bc_piece[:, 0:TPC], in_=fx[RK:96, 0:TPC],
              label=f"fix_{g}")
          w.wait("scalar", f"fix_{g}")
          w.dma("scalar", bounce_dtr[:, g * TPC:(g + 1) * TPC],
                dtrp[:, 0:TPC], label=f"dtrb_{g}", sem=f"qdtr{g % 2}")
          w.dma("scalar", bounce_bc[:, g * TPC:(g + 1) * TPC],
                bc_piece[:, 0:TPC], label=f"bcb_{g}", sem=f"qbcb{g % 2}")
          yield
          if p.dump:
              w.dma("scalar", dumps["dmp_dtr"][:, g * TPC:(g + 1) * TPC],
                    fx[0:RK, 0:TPC], sem="qdmp")
              w.dma("scalar", dumps["dmp_bc"][:, g * TPC:(g + 1) * TPC],
                    fx[RK:96, 0:TPC], label=f"dbc_{g}", sem="qdmp")
              w.wait("vector", f"dbc_{g}")


      for wi in range(NC) if p.cut >= 40 else []:
          w.wait("scalar", f"dtrb_{NC-2}")
          w.wait("scalar", f"dtrb_{NC-1}")
          if wi >= 2:
              w.wait("scalar", f"dt2_{(wi - 2) * HB + HB - 1}")
          w.dma("scalar", dtrp[:, 0:TPC] if False else sb["dtrw"][0:RK, (wi % 2) * TPC:(wi % 2) * TPC + TPC],
                bounce_dtr[:, wi * TPC:(wi + 1) * TPC], label=f"dtrl_{wi}",
                sem=f"qdl{wi % 2}")
          for hb in range(HB):
              idx = wi * HB + hb
              if idx == 0:
                  w.wait("tensor", f"fix_{NC-1}")
              else:
                  w.wait("tensor", f"dtw_{idx-1}")
              w.wait("tensor", f"dtrl_{wi}")
              for h in range(NSP):
                  cw = min(512, TPC - h * 512)
                  mm(ps[6 + h][:, 0:cw], wdt2b[0:RK, hb * 128:(hb + 1) * 128],
                     sb["dtrw"][0:RK, (wi % 2) * TPC + h * 512:
                                (wi % 2) * TPC + h * 512 + cw],
                     start=True, stop=True,
                     label=f"dt2_{idx}" if h == NSP - 1 else None)
              w.wait("scalar", f"dt2_{idx}")
              if idx >= 2:
                  w.wait("scalar", f"dtbo_{idx-2}")
              dsl = chunk(dtp, idx % 2, TPC)
              for h in range(NSP):
                  cw = min(512, TPC - h * 512)
                  cs = slice(h * 512, h * 512 + cw)
                  act(spl[:, cs], ps[6 + h][:, 0:cw], AF.Abs,
                      bias=bcol[:, hb:hb + 1])
                  act(spl[:, cs], spl[:, cs], AF.Exp, scale=-1.0)
                  act(spl[:, cs], spl[:, cs], AF.Ln, bias=1.0)
                  act(dsl[:, cs], ps[6 + h][:, 0:cw],
                      AF.Relu, bias=bcol[:, hb:hb + 1],
                      label=f"spx_{idx}" if h == NSP - 1 else None)
              w.wait("vector", f"spx_{idx}")
              vop("tensor_add", dsl, dsl, spl[:, 0:TPC], label=f"dtw_{idx}")
              w.wait("scalar", f"dtw_{idx}")
              w.dma("scalar", bounce_dt[hb * 128:(hb + 1) * 128,
                                        wi * TPC:(wi + 1) * TPC],
                    dsl, label=f"dtbo_{idx}", sem=f"qdt{idx % 2}")
              if p.dump:
                  w.dma("scalar", dumps["dmp_dt"][hb * 128:(hb + 1) * 128,
                                                  wi * TPC:(wi + 1) * TPC],
                        dsl, label=f"ddtd_{idx}", sem="qdmp")
                  w.wait("vector", f"ddtd_{idx}")
              yield

    n_scan = [0]

    def emit_scan_window(wi):
        first_of_batch = (wi % (NC // 2) == 0)
        for hb in range(HB):
            idx = wi * HB + hb
            if wi >= 2:
                w.wait("scalar", f"ywf_{(wi-2)*HB+hb}")
            w.wait("scalar", f"ubo_{NC*HB-2}")
            w.wait("scalar", f"ubo_{NC*HB-1}")
            w.dma("scalar", chunk(uw, idx % 2, TPC),
                  bounce_u[hb * 128:(hb + 1) * 128, wi * TPC:(wi + 1) * TPC],
                  label=f"uwl_{idx}", sem=f"quw{idx % 2}")
            w.wait("scalar", f"dtbo_{NC*HB-2}")
            w.wait("scalar", f"dtbo_{NC*HB-1}")
            if idx >= 2:
                w.wait("scalar", f"a_last_{idx-2}")
                w.wait("scalar", f"dtu_{idx-2}")
            w.dma("scalar", chunk(dt_sc, idx % 2, TPC),
                  bounce_dt[hb * 128:(hb + 1) * 128, wi * TPC:(wi + 1) * TPC],
                  label=f"dtl_{idx}", sem=f"qdw{idx % 2}")
            w.wait("vector", f"uwl_{idx}")
            w.wait("vector", f"dtl_{idx}")
            vop("tensor_mul", chunk(dtu_w, hb, TPC),
                chunk(dt_sc, idx % 2, TPC),
                chunk(uw, idx % 2, TPC))
            if wi >= 1:
                w.wait("vector", f"yout_{(wi-1)*HB+hb}")
            else:
                w.wait("vector", f"dtbo_{NC*HB-2}")
                w.wait("vector", f"dtbo_{NC*HB-1}")
            vop("memset",
                y_w[:, hb * TPC:(hb + 1) * TPC],
                0.0, label=f"dtu_{idx}")
            yield
        for n in range(NST):
            bi = wi * NST + n
            w.wait("scalar", f"bcb_{NC-2}")
            w.wait("scalar", f"bcb_{NC-1}")
            if bi >= 2:
                w.wait("scalar", f"bu_last_{bi-2}")
            w.dma("scalar", chunk(Bb, bi % 2, TPC),
                  bounce_bc[n:n + 1,
                            wi * TPC:(wi + 1) * TPC].broadcast_to([128, TPC]),
                  label=f"Bb_{bi}", sem=f"qBb{bi % 2}")
            if bi >= 2:
                w.wait("scalar", f"y_last_{bi-2}")
            w.dma("scalar", chunk(Cb, bi % 2, TPC),
                  bounce_bc[NST + n:NST + n + 1,
                            wi * TPC:(wi + 1) * TPC].broadcast_to([128, TPC]),
                  label=f"Cb_{bi}", sem=f"qCb{bi % 2}")
            for hb in range(HB):
                idx = wi * HB + hb
                si = n_scan[0]
                n_scan[0] += 1
                if si >= 2:
                    w.wait("scalar", f"scan_{si-2}")
                elif si == 0:
                    w.wait("scalar", "aneg_done")
                w.wait("scalar", f"dtl_{idx}")
                act(chunk(a_t, si % 2, TPC),
                    chunk(dt_sc, idx % 2, TPC),
                    AF.Exp, scale=aneg[:, hb * NST + n:hb * NST + n + 1],
                    label=(f"a_{si}" if not (n == NST - 1) else f"a_last_{idx}"))
                if n == NST - 1:
                    w.alias(f"a_{si}", f"a_last_{idx}")
                if si == 0:
                    w.wait("vector", "stats_out")
                    if p.dump:
                        w.wait("vector", "dstat")
                w.wait("vector", f"Bb_{bi}")
                w.wait("vector", f"dtu_{idx}")
                vop("tensor_mul", bu_t[:, 0:TPC], chunk(dtu_w, hb, TPC),
                    chunk(Bb, bi % 2, TPC),
                    label=f"bu_last_{bi}" if hb == HB - 1 else None)
                w.wait("vector", f"a_{si}")
                init = zcol[:, 0:1] if first_of_batch else \
                    carry[:, hb * NST + n:hb * NST + n + 1]

                def scn(e, si=si, init=init):
                    return e.tensor_tensor_scan(
                        out=st_t[:, 0:TPC],
                        data0=chunk(a_t, si % 2, TPC), data1=bu_t[:, 0:TPC],
                        initial=init, op0=ALU.mult, op1=ALU.add)
                w.emit("vector", scn, inc="v", label=f"scan_{si}")
                w.wait("vector", f"scan_{si}")
                vop("tensor_copy",
                    out=carry[:, hb * NST + n:hb * NST + n + 1],
                    in_=st_t[:, 0:TPC][:, TPC - 1:TPC])
                w.wait("vector", f"Cb_{bi}")
                vop("tensor_mul", yt_t[:, 0:TPC], st_t[:, 0:TPC],
                    chunk(Cb, bi % 2, TPC))
                vop("tensor_add",
                    y_w[:, hb * TPC:(hb + 1) * TPC],
                    y_w[:, hb * TPC:(hb + 1) * TPC],
                    yt_t[:, 0:TPC],
                    label=f"y_last_{bi}" if hb == HB - 1 else None)
                if hb == HB - 1:
                    yield
        for hb in range(HB):
            idx = wi * HB + hb
            ysl = y_w[:, hb * TPC:(hb + 1) * TPC]
            vop("scalar_tensor_tensor", ysl, chunk(uw, idx % 2, TPC),
                dcol[:, hb:hb + 1], ysl, ALU.mult, ALU.add,
                label=f"ywf_{idx}")
            w.wait("scalar", f"ywf_{idx}")
            w.dma("scalar", aa_in[wi, hb * 128:(hb + 1) * 128, :], ysl,
                  label=f"yout_{idx}", sem="qaa")
            if p.dump:
                w.dma("scalar",
                      dumps["dmp_ymix"][hb * 128:(hb + 1) * 128,
                                        wi * TPC:(wi + 1) * TPC], ysl,
                      label=f"dym_{idx}", sem="qdmp")
                w.wait("vector", f"dym_{idx}")
            yield

    slice_ctr = [0]

    def _emit_wslice(inap, width, tag):
        gi = slice_ctr[0]
        slice_ctr[0] += 1
        if gi >= 4:
            w.wait("gpsimd", f"wused_{gi-4}")
        w.dma("gpsimd", chunk(wbf, gi % 4, 2048)[:, 0:width], inap,
              label=f"wld_{gi}", sem=f"qw{gi % 4}")
        return gi

    def emit_ffn_expert(e_):
        nmj = MJS if e_ == NE else MJ
        shared = e_ == NE
        if not shared:
            w.wait("gpsimd", "g_out")
            if e_ >= 1:
                w.wait("gpsimd", f"gbc_used_{e_-1}")
            w.dma("gpsimd", g_bc[:, 0:TPC],
                  bounce_g[e_:e_ + 1, :].broadcast_to([128, TPC]),
                  label=f"gld_{e_}", sem="qgl")
        halves = range(2) if shared else range(1)
        for half in halves:
            mjs = range(half * MJ, half * MJ + MJ) if shared else range(MJ)
            for mj in mjs:
                mloc = mj - (half * MJ if shared else 0)
                if shared:
                    gin = sg_d[mj]
                    uin = su_d[mj]
                else:
                    gin = wg_d[e_, mj]
                    uin = wu_d[e_, mj]
                gi_g = _emit_wslice(gin, DC * 128, f"g{e_}_{mj}")
                gi_u = _emit_wslice(uin, DC * 128, f"u{e_}_{mj}")
                for tt in range(NTT):
                    uidx = f"{e_}_{tt}_{mj}"
                    pg = ps[(mloc % 3) * 2]
                    pu = ps[(mloc % 3) * 2 + 1]
                    tcs = slice(tt * TT, (tt + 1) * TT)
                    if tt >= 1:
                        w.wait("tensor", f"hq_{e_}_{tt-1}_{mj}")
                    if tt == 0:
                        w.wait("tensor", f"wld_{gi_g}")
                        w.wait("tensor", f"wld_{gi_u}")
                        prev_same = {0: 9, 1: 10, 2: 8}[mloc % 3]
                        if mloc >= 3:
                            w.wait("tensor", f"hq_{e_}_{NTT-1}_{mj-3}")
                        elif half == 1:
                            w.wait("tensor", f"hq_{e_}_{NTT-1}_{prev_same}")
                        elif e_ > 0:
                            pb = MJ if e_ - 1 == NE else 0
                            w.wait("tensor",
                                   f"hq_{e_-1}_{NTT-1}_{pb + prev_same}")
                        else:
                            w.wait("tensor", "var_done")
                    for c in range(DC):
                        mm(pg[:, 0:TT],
                           chunk(wbf, gi_g % 4, 2048)[:, c * 128:(c + 1) * 128],
                           chunk(xn, c, TPC)[:, tcs],
                           start=(c == 0), stop=(c == DC - 1),
                           label=f"hg_{uidx}" if c == DC - 1 else None)
                    for c in range(DC):
                        mm(pu[:, 0:TT],
                           chunk(wbf, gi_u % 4, 2048)[:, c * 128:(c + 1) * 128],
                           chunk(xn, c, TPC)[:, tcs],
                           start=(c == 0), stop=(c == DC - 1),
                           label=f"hu_{uidx}" if c == DC - 1 else None)
                    if tt == NTT - 1:
                        w.alias_late(f"wused_{gi_g}", f"hg_{uidx}")
                        w.alias_late(f"wused_{gi_u}", f"hu_{uidx}")
                    w.wait("scalar", f"hg_{uidx}")
                    sqi = (2 * mloc + tt) % 2
                    step = 2 * mloc + tt
                    if step >= 2:
                        pmj, ptt = divmod(step - 2, 2)
                        pmjg = pmj + (half * MJ if shared else 0)
                        w.wait("scalar", f"hq_{e_}_{ptt}_{pmjg}")
                    act(chunk(s_t, sqi, TT), pg[:, 0:TT], AF.Tanh, scale=ccst[:, 2:3])
                    act(chunk(s_t, sqi, TT), chunk(s_t, sqi, TT),
                        AF.Identity, scale=ccst[:, 2:3], bias=ccst[:, 2:3],
                        label=f"sil_{uidx}")
                    w.wait("vector", f"sil_{uidx}")
                    w.wait("vector", f"hu_{uidx}")
                    if mloc == 0 and tt == 0:
                        if shared and half == 1:
                            w.wait("vector", f"dn_last_{e_}_0")
                        elif e_ > 0:
                            w.wait("vector",
                                   f"dn_last_{e_-1}_{1 if e_-1 == NE else 0}")
                    hsl = hprime[:, (mloc * 2 + tt) * TT:(mloc * 2 + tt + 1) * TT]
                    if not shared:
                        if mj == 0 and tt == 0:
                            w.wait("vector", f"gld_{e_}")
                        vop("tensor_mul", chunk(q_t, sqi, TT), pu[:, 0:TT],
                            g_bc[:, tcs])
                        vop("tensor_mul", chunk(q_t, sqi, TT), pg[:, 0:TT],
                            chunk(q_t, sqi, TT))
                        vop("tensor_mul", hsl, chunk(s_t, sqi, TT),
                            chunk(q_t, sqi, TT), label=f"hq_{uidx}")
                    else:
                        vop("tensor_mul", chunk(q_t, sqi, TT), pu[:, 0:TT],
                            chunk(s_t, sqi, TT))
                        vop("tensor_mul", hsl, pg[:, 0:TT],
                            chunk(q_t, sqi, TT), label=f"hq_{uidx}")
                    if tt == NTT - 1:
                        yield
            yield "predown"
            last_mj = half * MJ + MJ - 1 if shared else MJ - 1
            w.wait("tensor", f"hq_{e_}_{NTT-1}_{last_mj}")
            for k in range(DC):
                if shared:
                    inap = sd_d[k][:, half * MJ * 128:(half * MJ + MJ) * 128]
                else:
                    inap = wd_d[e_, k][:, half * MJ * 128:(half * MJ + MJ) * 128]
                gi = _emit_wslice(inap, MJ * 128, f"d{e_}_{k}")
                w.wait("tensor", f"wld_{gi}")
                if e_ == 0 and half == 0 and k < 2:
                    w.wait("tensor", f"dtw_{NC*HB-1}")
                for tt in range(NTT):
                    prev_k = k - (2 if NTT == 1 else 1)
                    if prev_k >= 0:
                        w.wait("tensor", f"ytd_{e_}_{half}_{tt}_{prev_k}")
                    elif e_ > 0 or half == 1:
                        pe, ph = (e_, 0) if half == 1 else \
                            (e_ - 1, 1 if e_ - 1 == NE else 0)
                        pk = DC + prev_k
                        w.wait("tensor", f"ytd_{pe}_{ph}_{tt}_{pk}")
                for j in range(MJ):
                    for tt in range(NTT):
                        pd = ps[6 + tt % 2] if NTT > 1 else ps[6 + k % 2]
                        mm(pd[:, 0:TT],
                           chunk(wbf, gi % 4, 2048)[:, j * 128:(j + 1) * 128],
                           hprime[:, (j * 2 + tt) * TT:(j * 2 + tt + 1) * TT],
                           start=(j == 0), stop=(j == MJ - 1),
                           label=(f"dnm_{e_}_{half}_{tt}_{k}"
                                  if j == MJ - 1 else None))
                for tt in range(NTT):
                    pd = ps[6 + tt % 2] if NTT > 1 else ps[6 + k % 2]
                    tcs = slice(tt * TT, (tt + 1) * TT)
                    kidx = f"{e_}_{half}_{tt}_{k}"
                    w.wait("vector", f"dnm_{kidx}")
                    vop("tensor_add", chunk(y_tot, k, TPC)[:, tcs],
                        chunk(y_tot, k, TPC)[:, tcs], pd[:, 0:TT],
                        label=f"ytd_{kidx}")
                yield
                if True:
                    w.alias_late(f"wused_{gi}", f"dnm_{e_}_{half}_{NTT-1}_{k}")
            last = f"ytd_{e_}_{half}_{NTT-1}_{DC-1}"
            w.alias_late(f"dn_last_{e_}_{half}", last)
        if not shared:
            vop("engine_nop", label=f"gbc_used_{e_}")


    def gchain(gens):
        for g in gens:
            yield from g

    pgen = emit_p34()
    sgen = gchain([emit_scan_window(wi) for wi in range(NC)]) \
        if p.phase in ("all", "scan") else iter(())
    fgen = gchain([emit_ffn_expert(e_) for e_ in range(NE + 1)]) \
        if p.phase in ("all", "ffn") else iter(())
    if p.interleave:
        p_live = True
        f_hold = False
        while p_live:
            try:
                next(pgen)
            except StopIteration:
                p_live = False
            for _ in range(2):
                if f_hold:
                    break
                try:
                    if next(fgen) == "predown":
                        f_hold = True
                except StopIteration:
                    break
        F_TOT = (NE * (MJ + DC) + (MJS + 2 * DC)) or 1
        S_TOT = (NC * (2 + NST + 1)) or 1
        f_live = s_live = True
        fd = sd = 0
        while f_live or s_live:
            if s_live and (not f_live or fd * S_TOT >= sd * F_TOT):
                try:
                    next(sgen)
                    sd += 1
                except StopIteration:
                    s_live = False
            elif f_live:
                try:
                    next(fgen)
                    fd += 1
                except StopIteration:
                    f_live = False
    else:
        for _ in pgen:
            pass
        for _ in sgen:
            pass
        for _ in fgen:
            pass
    w.resolve_late()

    if p.phase in ("all", "scan"):
        for wi in range(NC):
            for hb in range(HB):
                w.wait("gpsimd", f"yout_{wi * HB + hb}")
        coll("AllToAll", ALU.bypass, [aa_in.opt()], [aa_out.opt()], "cc_aa")

    for c in range(DC):
        if c >= 2:
            w.wait("scalar", f"fin_{c-2}")
        if c == 0 and p.phase in ("all", "scan"):
            w.wait("scalar", f"ywf_{NC*HB-1}")
        w.dma("scalar", chunk(xfin, c % 2, TPC),
              x_own_tok[c * 128:(c + 1) * 128, :], label=f"xf_{c}",
              sem=f"qxf{c % 2}")
        if p.phase in ("all", "scan"):
            w.wait("scalar", "cc_aa")
            w.dma("scalar", chunk(ymr, c % 2, TPC),
                  aa_out[(c * 128) // CH,
                         (c * 128) % CH:(c * 128) % CH + 128, :],
                  label=f"fload_{c}", sem=f"qfl{c % 2}")
            w.wait("vector", f"fload_{c}")
        w.wait("vector", f"xf_{c}")
        if c >= 2:
            w.wait("vector", f"fin_{c-2}")
        if p.phase in ("all", "ffn"):
            w.wait("vector", f"ytd_{NE}_1_{NTT-1}_{c}")
        if p.phase == "all":
            vop("tensor_add", sq, chunk(ymr, c % 2, TPC),
                chunk(y_tot, c, TPC))
        elif p.phase == "scan":
            vop("tensor_copy", out=sq, in_=chunk(ymr, c % 2, TPC))
        elif p.phase == "ffn":
            vop("tensor_copy", out=sq, in_=chunk(y_tot, c, TPC))
        else:
            vop("memset", sq, 0.0)
        vop("tensor_add", chunk(ofin, c % 2, TPC), sq,
            chunk(xfin, c % 2, TPC), label=f"fadd_{c}")
        w.wait("scalar", f"fadd_{c}")
        w.dma("scalar", out_t[c * 128:(c + 1) * 128, :],
              chunk(ofin, c % 2, TPC), label=f"fin_{c}", sem=f"qout{c % 2}")
    w.wait("sync", f"fin_{DC-1}")

    def _final(e):
        for name, sem in w.sems.items():
            if name.startswith("q") and w.count[name] > 0:
                e.wait_ge(sem, w.count[name])
    w.plan["sync"].append(_final)

    with nc.Block() as block:
        @block.sync
        def _(eng):
            for f in w.plan["sync"]:
                f(eng)

        @block.scalar
        def _(eng):
            for f in w.plan["scalar"]:
                f(eng)

        @block.vector
        def _(eng):
            for f in w.plan["vector"]:
                f(eng)

        @block.tensor
        def _(eng):
            for f in w.plan["tensor"]:
                f(eng)

        @block.gpsimd
        def _(eng):
            for f in w.plan["gpsimd"]:
                f(eng)

    return nc


def _prep_inputs(p: P, inputs):
    x = np.ascontiguousarray(inputs["L0_attn_input"], dtype=np.float32)
    S, TPC = p.S, p.TPC
    xT = np.ascontiguousarray(x.reshape(S, D).T)
    wprojT = np.ascontiguousarray(
        np.concatenate([np.asarray(inputs["L0_attn_W_dt_in"]).T,
                        np.asarray(inputs["L0_attn_W_B"]).T,
                        np.asarray(inputs["L0_attn_W_C"]).T],
                       axis=1).astype(np.float32))
    routerT = np.ascontiguousarray(
        np.asarray(inputs["L0_ffn_router_w"]).T.astype(np.float32))
    wdt2T = np.ascontiguousarray(
        np.asarray(inputs["L0_attn_W_dt_out"]).T.astype(np.float32))
    eye128 = np.eye(128, dtype=np.float32)
    def tile_up(wm):
        h = wm.shape[1]
        return np.ascontiguousarray(
            np.asarray(wm, np.float32).reshape(DC, 128, h // 128, 128)
            .transpose(2, 1, 0, 3).reshape(h // 128, 128, DC * 128))

    def tile_dn(wm):
        h = wm.shape[0]
        return np.ascontiguousarray(
            np.asarray(wm, np.float32).reshape(h // 128, 128, DC, 128)
            .transpose(2, 1, 0, 3).reshape(DC, 128, h))

    wg_in = np.asarray(inputs["L0_ffn_wg"], np.float32)
    wu_in = np.asarray(inputs["L0_ffn_wu"], np.float32)
    wd_in = np.asarray(inputs["L0_ffn_wd"], np.float32)
    shared = dict(
        xT=xT, wprojT=wprojT, routerT=routerT, eye128=eye128,
        wg=np.stack([tile_up(wg_in[e]) for e in range(NE)]),
        wu=np.stack([tile_up(wu_in[e]) for e in range(NE)]),
        wd=np.stack([tile_dn(wd_in[e]) for e in range(NE)]),
        sg=tile_up(np.asarray(inputs["L0_ffn_sg"], np.float32)),
        su=tile_up(np.asarray(inputs["L0_ffn_su"], np.float32)),
        sd=tile_dn(np.asarray(inputs["L0_ffn_sd"], np.float32)),
    )
    alog = np.asarray(inputs["L0_attn_A_log"], np.float32)
    dv = np.asarray(inputs["L0_attn_D"], np.float32)
    bdt = np.asarray(inputs["L0_attn_b_dt_out"], np.float32)
    in_maps = []
    for c in range(NC):
        m = dict(shared)
        m["x_own_tok"] = np.ascontiguousarray(xT[:, c * TPC:(c + 1) * TPC])
        m["x_own_ch"] = np.ascontiguousarray(xT[c * CH:(c + 1) * CH, :])
        al = alog[c * CH:(c + 1) * CH].reshape(HB, 128, NST)
        m["alog_own"] = np.ascontiguousarray(
            al.transpose(1, 0, 2).reshape(128, HB * NST))
        m["d_own"] = np.ascontiguousarray(
            dv[c * CH:(c + 1) * CH].reshape(HB, 128).T)
        m["bdt_own"] = np.ascontiguousarray(
            bdt[c * CH:(c + 1) * CH].reshape(HB, 128).T)
        m["wdt2T_own"] = np.ascontiguousarray(wdt2T[:, c * CH:(c + 1) * CH])
        in_maps.append(m)
    return in_maps


def kernel(**inputs):
    p = P()
    in_maps = _prep_inputs(p, inputs)
    nc = build(p)
    res = run_bass_kernel_spmd(nc, in_maps, core_ids=list(range(NC)))
    outT = np.concatenate([res.results[c]["out"] for c in range(NC)], axis=1)
    return np.ascontiguousarray(outT.T).reshape(2, p.S // 2, D)


# version 07
# speedup vs baseline: 1.1261x; 1.1261x over previous
import numpy as np
from dataclasses import dataclass
from contextlib import ExitStack

import concourse.bass as bass
import concourse.mybir as mybir
from concourse.bass_utils import run_bass_kernel_spmd
from concourse.aot_env import allow_ici_all_to_all

F32 = mybir.dt.float32
BF16 = mybir.dt.bfloat16
AF = mybir.ActivationFunctionType
ALU = mybir.AluOpType

NC = 8
D = 2048
DC = 16
NST = 16
RK = 64
NE = 8
HID = 1408
MJ = 11
HS = 2816
MJS = 22
CH = 256
HB = 2
EPS = 1e-5


@dataclass
class P:
    TPC: int = 1024
    dump: bool = False
    interleave: bool = True
    phase: str = "all"
    cut: int = 99

    @property
    def S(self):
        return NC * self.TPC

    @property
    def TT(self):
        return min(512, self.TPC)

    @property
    def NTT(self):
        return max(1, self.TPC // self.TT)


class Wk:
    ENGS = ("sync", "scalar", "vector", "tensor", "gpsimd")

    def __init__(self, nc, ctx):
        self.nc = nc
        self.ctx = ctx
        self.plan = {e: [] for e in self.ENGS}
        self.sems = {}
        self.count = {}
        self.labels = {}
        for name in ("v", "s", "t", "g", "cc"):
            self._mksem(name)

    def _mksem(self, name):
        if name not in self.sems:
            self.sems[name] = self.ctx.enter_context(self.nc.semaphore(name))
            self.count[name] = 0
        return self.sems[name]

    def emit(self, eng, fn, inc=None, amt=1, label=None):
        if inc is not None:
            self.count[inc] += amt
            if label is not None:
                self.labels[label] = (inc, self.count[inc])
            sem = self.sems[inc]

            def wrapped(e, fn=fn, sem=sem, amt=amt):
                fn(e).then_inc(sem, amt)
            self.plan[eng].append(wrapped)
        else:
            assert label is None
            self.plan[eng].append(fn)

    def alias(self, new, old):
        self.labels[new] = self.labels[old]

    def alias_late(self, new, old):
        self._late = getattr(self, "_late", [])
        self._late.append((new, old))

    def resolve_late(self):
        for new, old in getattr(self, "_late", []):
            self.labels[new] = self.labels[old]

    def wait(self, eng, label):
        def wrapped(e, label=label):
            name, val = self.labels[label]
            e.wait_ge(self.sems[name], val)
        self.plan[eng].append(wrapped)

    def dma(self, queue_eng, out, in_, label=None, sem="q"):
        self._mksem(sem)
        if sem == "qdmp" and self.count[sem] > 0:
            val = self.count[sem]

            def wr(e, val=val):
                e.wait_ge(self.sems["qdmp"], val)
            self.plan[queue_eng].append(wr)
        self.emit(queue_eng, lambda e, o=out, i=in_: e.dma_start(out=o, in_=i),
                  inc=sem, amt=16, label=label)


def build(p: P):
    nc = bass.Bass("TRN2", target_bir_lowering=False,
                   detect_race_conditions=False)
    TPC, S, TT, NTT = p.TPC, p.S, p.TT, p.NTT
    TSUB = TPC // 128
    NSP = (TPC + 511) // 512
    ctx = ExitStack()
    nc._build_ctx = ctx

    def din(name, shape):
        return nc.dram_tensor(name, shape, F32, kind="ExternalInput").ap()

    xT = din("xT", [D, S])
    x_own_tok = din("x_own_tok", [D, TPC])
    x_own_ch = din("x_own_ch", [CH, S])
    alog_own = din("alog_own", [128, HB * NST])
    d_own = din("d_own", [128, HB])
    bdt_own = din("bdt_own", [128, HB])
    wdt2T_own = din("wdt2T_own", [RK, CH])
    wprojT = din("wprojT", [D, RK + 2 * NST])
    routerT = din("routerT", [D, NE])
    wg_d = din("wg", [NE, MJ, 128, DC * 128])
    wu_d = din("wu", [NE, MJ, 128, DC * 128])
    wd_d = din("wd", [NE, DC, 128, MJ * 128])
    sg_d = din("sg", [MJS, 128, DC * 128])
    su_d = din("su", [MJS, 128, DC * 128])
    sd_d = din("sd", [DC, 128, MJS * 128])
    eye = din("eye128", [128, 128])

    out_t = nc.dram_tensor("out", [D, TPC], F32, kind="ExternalOutput").ap()
    dumps = {}
    if p.dump:
        for nm, sh, ddt in [("dmp_stats", [2, TPC], F32),
                            ("dmp_dtr", [RK, S], F32),
                            ("dmp_bc", [2 * NST, S], F32),
                            ("dmp_u", [CH, S], F32),
                            ("dmp_dt", [CH, S], BF16),
                            ("dmp_ymix", [CH, S], BF16),
                            ("dmp_gates", [NE, TPC], F32),
                            ("dmp_xn", [D, TPC], F32)]:
            dumps[nm] = nc.dram_tensor(nm, sh, ddt, kind="ExternalOutput").ap()

    stats_mine = nc.dram_tensor("stats_mine", [2, TPC], F32).ap()
    stats_all = nc.dram_tensor("stats_all", [NC, 2, TPC], F32).ap()
    bounce_bc = nc.dram_tensor("bounce_bc", [2 * NST, S], BF16).ap()
    bounce_g = nc.dram_tensor("bounce_g", [NE, TPC], F32).ap()
    bounce_u = nc.dram_tensor("bounce_u", [CH, S], BF16).ap()
    bounce_dtr = nc.dram_tensor("bounce_dtr", [RK, S], BF16).ap()
    bounce_dt = nc.dram_tensor("bounce_dt", [CH, S], BF16).ap()
    aa_in = nc.dram_tensor("aa_in", [NC, CH, TPC], BF16).ap()
    aa_out = nc.dram_tensor("aa_out", [NC, CH, TPC], BF16).ap()

    sb = {}

    def t(name, shape, dt=F32):
        sb[name] = ctx.enter_context(nc.sbuf_tensor(name, shape, dt))
        return sb[name]

    xn = t("xn", [128, DC * TPC], BF16)
    y_tot = t("y_tot", [128, DC * TPC], BF16)
    dtrp = t("dtrp", [RK, TPC], BF16)
    dt_sc = t("dt_sc", [128, 2 * TPC], BF16)
    t("dtrw", [RK, 2 * TPC], BF16)
    cstage = t("cstage", [128, 2048], F32)
    wprojb = t("wprojb", [128, DC * 96], BF16)
    routb = t("routb", [128, DC * NE], BF16)
    wdt2b = t("wdt2b", [RK, CH], BF16)
    aneg = t("aneg", [128, HB * NST], F32)
    dcol = t("dcol", [128, HB], F32)
    bcol = t("bcol", [128, HB], F32)
    wsum_neg = t("wsum_neg", [96, 1], F32)
    eye_sb = t("eye_sb", [128, 128], F32)
    ones_b = t("ones_b", [128, 1], BF16)
    ones_f = t("ones_f", [128, 1], F32)
    zcol = t("zcol", [128, 1], F32)
    carry = t("carry", [128, HB * NST], F32)
    ccst = t("ccst", [128, 4], F32)

    gtmp = t("gtmp", [128, 6 * NE], F32)
    gates_fm = t("gates_fm", [NE, TPC], F32)

    xo = t("xo", [128, 2 * TPC], F32)
    xpieceb = t("xpieceb", [128, 4 * TPC], BF16)
    ar1 = t("ar1", [128, 4 * TPC], F32)
    mb = ar1[:, 0:TPC]
    rsb = ar1[:, TPC:2 * TPC]
    sq = ar1[:, 2 * TPC:3 * TPC]
    fx = ar1[0:96, 3 * TPC:4 * TPC]
    bc_piece = t("bc_piece", [2 * NST, TPC], BF16)

    a_t = cstage if 2 * TPC <= 2048 else t("a_t", [128, 2 * TPC], F32)
    bu_t = t("bu_t", [128, TPC], F32)
    yt_t = t("yt_t", [128, TPC], BF16)
    mrow = bu_t[0:1, 0:TPC]
    rsrow = cstage[0:1, TPC:2 * TPC]
    srow = cstage[0:1, 0:TPC]
    st_t = t("st_t", [128, TPC], BF16)
    Bb = t("Bb", [128, 2 * TPC], BF16)
    Cb = t("Cb", [128, 2 * TPC], BF16)
    spl = t("spl", [128, TPC], BF16)
    dtu_w = t("dtu_w", [128, HB * TPC], BF16)
    uw = t("uw", [128, 2 * TPC], BF16)
    y_w = t("y_w", [128, HB * TPC], BF16)
    dtp = y_w

    wbf = t("wbf", [128, 4 * 2048], BF16)
    hprime = t("hprime", [128, MJ * 2 * TT], BF16)
    s_t = t("s_t", [128, 2 * TT], BF16)
    q_t = t("q_t", [128, 2 * TT], BF16)
    g_bc = t("g_bc", [128, TPC], BF16)

    ymr = xpieceb
    xfin = xo
    ofin = ar1[:, 0:2 * TPC]

    ps = [ctx.enter_context(nc.psum_tensor(f"ps{i}", [128, 512], F32))
          for i in range(8)]

    w = Wk(nc, ctx)

    def act(out, in_, func, bias=0.0, scale=1.0, label=None):
        def fn(e, out=out, in_=in_, func=func, bias=bias, scale=scale):
            return e.activation(out=out, in_=in_, func=func, bias=bias,
                                scale=scale)
        w.emit("scalar", fn, inc="s" if label else None, label=label)

    _vs = [0]

    def vop(name, *args, eng="vector", label=None, **kw):
        def fn(e, name=name, args=args, kw=kw):
            return getattr(e, name)(*args, **kw)
        w.emit(eng, fn, inc=("v" if eng == "vector" else "g") if label else None,
               label=label)

    def vsync(name, *args, **kw):
        _vs[0] += 1
        lbl = f"_vs{_vs[0]}"
        vop(name, *args, label=lbl, **kw)
        w.wait("vector", lbl)

    def mm(out, lhsT, rhs, start, stop, label=None):
        def fn(e, out=out, lhsT=lhsT, rhs=rhs, start=start, stop=stop):
            return e.matmul(out, lhsT, rhs, start=start, stop=stop)
        w.emit("tensor", fn, inc="t" if label else None, label=label)

    def chunk(tile, i, width, rows=128):
        return tile[0:rows, i * width:(i + 1) * width]

    w.dma("sync", cstage[:, 0:DC * 96],
          wprojT.rearrange("(c p) k -> p c k", p=128), sem="qc")
    w.dma("sync", cstage[:, DC * 96:DC * 96 + DC * NE],
          routerT.rearrange("(c p) k -> p c k", p=128), sem="qc")
    w.dma("sync", cstage[0:128, 1664:1664 + HB * NST], alog_own, sem="qc")
    w.dma("sync", cstage[0:RK, 1696:1696 + CH], wdt2T_own, sem="qc")
    w.dma("sync", eye_sb[:], eye, sem="qc")
    w.dma("sync", dcol[:], d_own, sem="qc")
    w.dma("sync", bcol[:], bdt_own, label="ld_const", sem="qc")

    w.wait("vector", "ld_const")
    vop("memset", ccst[:, 0:1], EPS)
    vop("memset", ccst[:, 1:2], -0.5)
    vop("memset", ccst[:, 2:3], 0.5)
    vop("memset", ones_b[:], 1.0)
    vop("memset", ones_f[:], 1.0)
    vop("memset", zcol[:], 0.0)
    vop("tensor_copy", out=wprojb[:], in_=cstage[:, 0:DC * 96])
    vop("tensor_copy", out=routb[:], in_=cstage[:, DC * 96:DC * 96 + DC * NE])
    vop("tensor_copy", out=wdt2b[:], in_=cstage[0:RK, 1696:1696 + CH])
    for i in range(DC):
        vop("memset", chunk(y_tot, i, TPC), 0.0)
    vop("engine_nop", label="consts_cast")
    w.wait("scalar", "ld_const")
    act(aneg[:], cstage[0:128, 1664:1664 + HB * NST], AF.Exp, label="an1")
    w.wait("vector", "an1")
    vop("tensor_scalar_mul", aneg[:], aneg[:], -1.0, label="aneg_done")

    w.wait("tensor", "consts_cast")
    for c in range(DC):
        mm(ps[6][0:96, 0:1], chunk(wprojb, c, 96), ones_b[:],
           start=(c == 0), stop=(c == DC - 1),
           label="wsum_mm" if c == DC - 1 else None)
    w.wait("vector", "wsum_mm")
    vop("tensor_scalar_mul", wsum_neg[:], ps[6][0:96, 0:1], -1.0,
        label="wsum_done")

    for i in range(DC) if p.cut >= 1 else []:
        if i >= 2:
            w.wait("scalar", f"stat_mm_{i-2}")
        w.dma("scalar", chunk(xo, i % 2, TPC),
              x_own_tok[i * 128:(i + 1) * 128, :], label=f"xo_{i}",
              sem=f"qxo{i % 2}")
        w.wait("scalar", f"xo_{i}")
        if i >= 1:
            w.wait("scalar", f"stat_mm_{i-1}")
        act(sq, chunk(xo, i % 2, TPC), AF.Square, label=f"sqr_{i}")
        if i == 0:
            w.wait("tensor", "wsum_done")
        w.wait("tensor", f"sqr_{i}")
        for h in range(NSP):
            cw = min(512, TPC - h * 512)
            mm(ps[4 + h][0:1, 0:cw], ones_f[:],
               chunk(xo, i % 2, TPC)[:, h * 512:h * 512 + cw],
               start=(i == 0), stop=(i == DC - 1))
            mm(ps[6 + h][0:1, 0:cw], ones_f[:], sq[:, h * 512:h * 512 + cw],
               start=(i == 0), stop=(i == DC - 1),
               label=f"stat_mm_{i}" if h == NSP - 1 else None)
    if p.cut >= 1:
      w.wait("vector", f"stat_mm_{DC-1}")
    for h in range(NSP):
        cw = min(512, TPC - h * 512)
        vop("tensor_scalar_mul", mrow[:, h * 512:h * 512 + cw],
            ps[4 + h][0:1, 0:cw], 1.0 / D)
        vop("tensor_scalar_mul", srow[:, h * 512:h * 512 + cw],
            ps[6 + h][0:1, 0:cw], 1.0 / D,
            label="stats_evac" if h == NSP - 1 else None)
    vop("tensor_mul", sq[0:1, 0:TPC], mrow, mrow)
    vop("tensor_sub", srow, srow, sq[0:1, 0:TPC],
        label="var_done")
    w.wait("scalar", "var_done")
    act(srow, srow, AF.Ln, bias=ccst[0:1, 0:1],
        label="lnv")
    act(rsrow, srow, AF.Exp, scale=ccst[0:1, 1:2],
        label="rs_done")
    w.wait("scalar", "rs_done")
    w.dma("scalar", stats_mine[0:1, :], mrow, sem="qst")
    w.dma("scalar", stats_mine[1:2, :], rsrow, label="stats_out", sem="qst")
    if p.dump:
        w.dma("scalar", dumps["dmp_stats"][0:1, :], mrow, sem="qdmp")
        w.dma("scalar", dumps["dmp_stats"][1:2, :], rsrow, label="dstat",
              sem="qdmp")

    def coll(kind, op, ins, outs, label):
        def fn(e, kind=kind, op=op, ins=ins, outs=outs):
            if kind == "AllToAll":
                with allow_ici_all_to_all():
                    return e.collective_compute(
                        kind, op, replica_groups=[list(range(NC))],
                        ins=ins, outs=outs)
            return e.collective_compute(
                kind, op, replica_groups=[list(range(NC))], ins=ins, outs=outs)
        w.emit("gpsimd", fn, inc="cc", label=label)

    if p.cut >= 2:
        w.wait("gpsimd", "stats_out")
        coll("AllGather", ALU.bypass, [stats_mine.opt()], [stats_all.opt()],
             "cc_stats")

    for i in range(DC) if p.cut >= 3 else []:
        if i >= 2:
            w.wait("scalar", f"xn_{i-2}")
        w.dma("scalar", chunk(xo, i % 2, TPC),
              x_own_tok[i * 128:(i + 1) * 128, :], label=f"xo2_{i}",
              sem=f"qxo{i % 2}")
        if i == 0:
            w.wait("scalar", "rs_done")
            w.dma("scalar", mb,
                  stats_mine[0:1, :].broadcast_to([128, TPC]), sem="qmr")
            w.dma("scalar", rsb,
                  stats_mine[1:2, :].broadcast_to([128, TPC]), label="ownbc",
                  sem="qmr")
        w.wait("vector", f"xo2_{i}")
        if i == 0:
            w.wait("vector", "ownbc")
        vop("tensor_sub", sq, chunk(xo, i % 2, TPC), mb)
        vop("tensor_mul", chunk(xn, i, TPC), sq, rsb,
            label=f"xn_{i}")

    w.wait("tensor", f"xn_{DC-1}") if p.cut >= 31 else None
    w.wait("tensor", "var_done") if p.cut >= 31 else None
    for ts in range(TSUB) if p.cut >= 31 else []:
        if ts >= 1:
            w.wait("tensor", f"rlec_{ts-1}")
        for c in range(DC):
            mm(ps[6][:, ts * NE:(ts + 1) * NE],
               chunk(xn, c, TPC)[:, ts * 128:(ts + 1) * 128],
               chunk(routb, c, NE), start=(c == 0), stop=(c == DC - 1),
               label=f"rl_{ts}" if c == DC - 1 else None)
        w.wait("vector", f"rl_{ts}")
        L = gtmp[:, 0:NE]
        vop("tensor_copy", out=L, in_=ps[6][:, ts * NE:(ts + 1) * NE],
            label=f"rlec_{ts}")
        w.wait("vector", f"rlec_{ts}")
        if p.cut == 31:
            w.alias(f"gw_{ts}", f"rlec_{ts}")
            continue
        mx = gtmp[:, NE:NE + 1]
        vsync("reduce_max", mx, L, axis=mybir.AxisListType.X)
        vop("tensor_scalar_mul", mx, mx, -1.0, label=f"gmx_{ts}")
        w.wait("vector", f"gmx_{ts}")
        w.wait("scalar", f"gmx_{ts}")
        act(L, L, AF.Exp, bias=mx, label=f"gex_{ts}")
        w.wait("vector", f"gex_{ts}")
        m1 = gtmp[:, NE + 1:NE + 2]
        vsync("reduce_max", m1, L, axis=mybir.AxisListType.X)
        msk = gtmp[:, 2 * NE:3 * NE]
        vsync("tensor_scalar", msk, L, m1, None, ALU.is_ge)
        vsync("tensor_scalar", msk, msk, -1.0, 1.0, ALU.mult, ALU.add)
        g2 = gtmp[:, 3 * NE:4 * NE]
        vsync("tensor_mul", g2, L, msk)
        m2 = gtmp[:, NE + 2:NE + 3]
        vsync("reduce_max", m2, g2, axis=mybir.AxisListType.X)
        vsync("tensor_scalar", msk, L, m2, None, ALU.is_ge)
        den = gtmp[:, NE + 3:NE + 4]
        vsync("tensor_add", den, m1, m2)
        rec = gtmp[:, NE + 4:NE + 5]
        vsync("reciprocal", rec, den)
        vsync("tensor_mul", L, L, msk)
        wslot = gtmp[:, (4 + (ts % 2)) * NE:(5 + (ts % 2)) * NE]
        vop("tensor_scalar_mul", wslot, L, rec, label=f"gw_{ts}")
        w.wait("vector", f"gw_{ts}")
        if p.cut == 32:
            continue
        w.wait("tensor", f"gw_{ts}")
        if ts >= 1:
            w.wait("tensor", f"gfm_{ts-1}")

        def tp(e, wslot=wslot):
            return e.transpose(ps[7][0:NE, 0:128], wslot, eye_sb[:])
        w.emit("tensor", tp, inc="t", label=f"gt_{ts}")
        w.wait("vector", f"gt_{ts}")
        vop("tensor_copy", out=gates_fm[:, ts * 128:(ts + 1) * 128],
            in_=ps[7][0:NE, 0:128], label=f"gfm_{ts}")
    if p.cut >= 34:
        w.wait("scalar", f"gfm_{TSUB-1}")
        w.dma("scalar", bounce_g[:, :], gates_fm[:, 0:TPC], label="g_out",
              sem="qgo")
    if p.dump and p.cut >= 34:
        w.dma("scalar", dumps["dmp_gates"], gates_fm[:, 0:TPC], sem="qdmp")
        for i in range(DC):
            w.wait("vector", f"xn_{i}")
            vop("tensor_copy", out=sq, in_=chunk(xn, i, TPC),
                label=f"dxn_{i}")
            w.wait("scalar", f"dxn_{i}")
            w.dma("scalar", dumps["dmp_xn"][i * 128:(i + 1) * 128, :],
                  sq, label=f"dxnd_{i}", sem="qdmp")
            w.wait("vector", f"dxnd_{i}")

    def emit_p34():
      for g in range(NC) if p.cut >= 40 else []:
          w.wait("scalar", "cc_stats")
          if g == 0:
              w.wait("scalar", f"xn_{DC-1}")
          else:
              w.wait("scalar", f"fix_{g-1}")
          w.dma("scalar", mb,
                stats_all[g, 0:1, :].broadcast_to([128, TPC]), sem="qmr")
          w.dma("scalar", rsb,
                stats_all[g, 1:2, :].broadcast_to([128, TPC]), label=f"gbc_{g}",
                sem="qmr")
          for hb in range(HB):
              idx = g * HB + hb
              if idx >= 2:
                  w.wait("scalar", f"u_{idx-2}")
              elif g == 0:
                  w.wait("scalar", f"xn_{DC-1}")
              w.dma("scalar", chunk(xo, idx % 2, TPC),
                    x_own_ch[hb * 128:(hb + 1) * 128, g * TPC:(g + 1) * TPC],
                    label=f"xch_{idx}", sem=f"qxo{idx % 2}")
              w.wait("vector", f"xch_{idx}")
              if hb == 0:
                  w.wait("vector", f"gbc_{g}")
              if idx >= 2:
                  w.wait("vector", f"ubo_{idx-2}")
              vop("tensor_sub", sq, chunk(xo, idx % 2, TPC),
                  mb)
              vop("tensor_mul", chunk(uw, idx % 2, TPC), sq,
                  rsb, label=f"u_{idx}")
              w.wait("scalar", f"u_{idx}")
              w.dma("scalar",
                    bounce_u[hb * 128:(hb + 1) * 128, g * TPC:(g + 1) * TPC],
                    chunk(uw, idx % 2, TPC), label=f"ubo_{idx}",
                    sem=f"qub{idx % 2}")
              if p.dump:
                  vop("tensor_copy", out=sq,
                      in_=chunk(uw, idx % 2, TPC), label=f"du_{idx}")
                  w.wait("scalar", f"du_{idx}")
                  w.dma("scalar",
                        dumps["dmp_u"][hb * 128:(hb + 1) * 128,
                                       g * TPC:(g + 1) * TPC],
                        sq, label=f"dud_{idx}", sem="qdmp")
                  w.wait("vector", f"dud_{idx}")
          for c in range(DC):
              idx = g * DC + c
              if idx >= 4:
                  w.wait("gpsimd", f"pmm_{idx-4}")
              w.dma("gpsimd", chunk(xpieceb, idx % 4, TPC),
                    xT[c * 128:(c + 1) * 128, g * TPC:(g + 1) * TPC],
                    label=f"xp_{idx}", sem=f"qxp{idx % 4}")
              w.wait("tensor", f"xp_{idx}")
              if c == 0:
                  w.wait("tensor", "wsum_done")
                  w.wait("tensor", f"gfm_{TSUB-1}")
                  if g >= 1:
                      w.wait("tensor", f"fix_{g-1}")
              for h in range(NSP):
                  cw = min(512, TPC - h * 512)
                  mm(ps[6 + h][0:96, 0:cw], chunk(wprojb, c, 96),
                     chunk(xpieceb, idx % 4, TPC)[:, h * 512:h * 512 + cw],
                     start=(c == 0), stop=(c == DC - 1),
                     label=f"pmm_{idx}" if h == NSP - 1 else None)
              if c == DC - 1:
                  w.alias(f"prj_{g}", f"pmm_{idx}")
          w.wait("vector", f"prj_{g}")
          for h in range(NSP):
              cw = min(512, TPC - h * 512)
              cs = slice(h * 512, h * 512 + cw)
              vop("scalar_tensor_tensor", fx[0:96, cs], mb[0:96, cs],
                  wsum_neg[0:96, 0:1], ps[6 + h][0:96, 0:cw], ALU.mult, ALU.add)
          vop("tensor_mul", fx[0:96, 0:TPC], fx[0:96, 0:TPC], rsb[0:96, :])
          if g >= 1:
              w.wait("vector", f"dtrb_{g-1}")
              w.wait("vector", f"bcb_{g-1}")
          vop("tensor_copy", out=dtrp[:, 0:TPC], in_=fx[0:RK, 0:TPC])
          vop("tensor_copy", out=bc_piece[:, 0:TPC], in_=fx[RK:96, 0:TPC],
              label=f"fix_{g}")
          w.wait("scalar", f"fix_{g}")
          w.dma("scalar", bounce_dtr[:, g * TPC:(g + 1) * TPC],
                dtrp[:, 0:TPC], label=f"dtrb_{g}", sem=f"qdtr{g % 2}")
          w.dma("scalar", bounce_bc[:, g * TPC:(g + 1) * TPC],
                bc_piece[:, 0:TPC], label=f"bcb_{g}", sem=f"qbcb{g % 2}")
          yield
          if p.dump:
              w.dma("scalar", dumps["dmp_dtr"][:, g * TPC:(g + 1) * TPC],
                    fx[0:RK, 0:TPC], sem="qdmp")
              w.dma("scalar", dumps["dmp_bc"][:, g * TPC:(g + 1) * TPC],
                    fx[RK:96, 0:TPC], label=f"dbc_{g}", sem="qdmp")
              w.wait("vector", f"dbc_{g}")


      for wi in range(NC) if p.cut >= 40 else []:
          w.wait("scalar", f"dtrb_{NC-2}")
          w.wait("scalar", f"dtrb_{NC-1}")
          if wi >= 2:
              w.wait("scalar", f"dt2_{(wi - 2) * HB + HB - 1}")
          w.dma("scalar", dtrp[:, 0:TPC] if False else sb["dtrw"][0:RK, (wi % 2) * TPC:(wi % 2) * TPC + TPC],
                bounce_dtr[:, wi * TPC:(wi + 1) * TPC], label=f"dtrl_{wi}",
                sem=f"qdl{wi % 2}")
          for hb in range(HB):
              idx = wi * HB + hb
              if idx == 0:
                  w.wait("tensor", f"fix_{NC-1}")
              else:
                  w.wait("tensor", f"dtw_{idx-1}")
              w.wait("tensor", f"dtrl_{wi}")
              for h in range(NSP):
                  cw = min(512, TPC - h * 512)
                  mm(ps[6 + h][:, 0:cw], wdt2b[0:RK, hb * 128:(hb + 1) * 128],
                     sb["dtrw"][0:RK, (wi % 2) * TPC + h * 512:
                                (wi % 2) * TPC + h * 512 + cw],
                     start=True, stop=True,
                     label=f"dt2_{idx}" if h == NSP - 1 else None)
              w.wait("scalar", f"dt2_{idx}")
              if idx >= 2:
                  w.wait("scalar", f"dtbo_{idx-2}")
              dsl = chunk(dtp, idx % 2, TPC)
              for h in range(NSP):
                  cw = min(512, TPC - h * 512)
                  cs = slice(h * 512, h * 512 + cw)
                  act(spl[:, cs], ps[6 + h][:, 0:cw], AF.Abs,
                      bias=bcol[:, hb:hb + 1])
                  act(spl[:, cs], spl[:, cs], AF.Exp, scale=-1.0)
                  act(spl[:, cs], spl[:, cs], AF.Ln, bias=1.0)
                  act(dsl[:, cs], ps[6 + h][:, 0:cw],
                      AF.Relu, bias=bcol[:, hb:hb + 1],
                      label=f"spx_{idx}" if h == NSP - 1 else None)
              w.wait("vector", f"spx_{idx}")
              vop("tensor_add", dsl, dsl, spl[:, 0:TPC], label=f"dtw_{idx}")
              w.wait("scalar", f"dtw_{idx}")
              w.dma("scalar", bounce_dt[hb * 128:(hb + 1) * 128,
                                        wi * TPC:(wi + 1) * TPC],
                    dsl, label=f"dtbo_{idx}", sem=f"qdt{idx % 2}")
              if p.dump:
                  w.dma("scalar", dumps["dmp_dt"][hb * 128:(hb + 1) * 128,
                                                  wi * TPC:(wi + 1) * TPC],
                        dsl, label=f"ddtd_{idx}", sem="qdmp")
                  w.wait("vector", f"ddtd_{idx}")
              yield

    n_scan = [0]

    def emit_scan_window(wi):
        first_of_batch = (wi % (NC // 2) == 0)
        for hb in range(HB):
            idx = wi * HB + hb
            if wi >= 2:
                w.wait("scalar", f"ywf_{(wi-2)*HB+hb}")
            w.wait("scalar", f"ubo_{NC*HB-2}")
            w.wait("scalar", f"ubo_{NC*HB-1}")
            w.dma("scalar", chunk(uw, idx % 2, TPC),
                  bounce_u[hb * 128:(hb + 1) * 128, wi * TPC:(wi + 1) * TPC],
                  label=f"uwl_{idx}", sem=f"quw{idx % 2}")
            w.wait("scalar", f"dtbo_{NC*HB-2}")
            w.wait("scalar", f"dtbo_{NC*HB-1}")
            if idx >= 2:
                w.wait("scalar", f"a_last_{idx-2}")
                w.wait("scalar", f"dtu_{idx-2}")
            w.dma("scalar", chunk(dt_sc, idx % 2, TPC),
                  bounce_dt[hb * 128:(hb + 1) * 128, wi * TPC:(wi + 1) * TPC],
                  label=f"dtl_{idx}", sem=f"qdw{idx % 2}")
            w.wait("vector", f"uwl_{idx}")
            w.wait("vector", f"dtl_{idx}")
            vop("tensor_mul", chunk(dtu_w, hb, TPC),
                chunk(dt_sc, idx % 2, TPC),
                chunk(uw, idx % 2, TPC))
            if wi >= 1:
                w.wait("vector", f"yout_{(wi-1)*HB+hb}")
            else:
                w.wait("vector", f"dtbo_{NC*HB-2}")
                w.wait("vector", f"dtbo_{NC*HB-1}")
            vop("memset",
                y_w[:, hb * TPC:(hb + 1) * TPC],
                0.0, label=f"dtu_{idx}")
            yield
        for n in range(NST):
            bi = wi * NST + n
            w.wait("scalar", f"bcb_{NC-2}")
            w.wait("scalar", f"bcb_{NC-1}")
            if bi >= 2:
                w.wait("scalar", f"bu_last_{bi-2}")
            w.dma("scalar", chunk(Bb, bi % 2, TPC),
                  bounce_bc[n:n + 1,
                            wi * TPC:(wi + 1) * TPC].broadcast_to([128, TPC]),
                  label=f"Bb_{bi}", sem=f"qBb{bi % 2}")
            if bi >= 2:
                w.wait("scalar", f"y_last_{bi-2}")
            w.dma("scalar", chunk(Cb, bi % 2, TPC),
                  bounce_bc[NST + n:NST + n + 1,
                            wi * TPC:(wi + 1) * TPC].broadcast_to([128, TPC]),
                  label=f"Cb_{bi}", sem=f"qCb{bi % 2}")
            for hb in range(HB):
                idx = wi * HB + hb
                si = n_scan[0]
                n_scan[0] += 1
                if si >= 2:
                    w.wait("scalar", f"scan_{si-2}")
                elif si == 0:
                    w.wait("scalar", "aneg_done")
                w.wait("scalar", f"dtl_{idx}")
                act(chunk(a_t, si % 2, TPC),
                    chunk(dt_sc, idx % 2, TPC),
                    AF.Exp, scale=aneg[:, hb * NST + n:hb * NST + n + 1],
                    label=(f"a_{si}" if not (n == NST - 1) else f"a_last_{idx}"))
                if n == NST - 1:
                    w.alias(f"a_{si}", f"a_last_{idx}")
                if si == 0:
                    w.wait("vector", "stats_out")
                    if p.dump:
                        w.wait("vector", "dstat")
                w.wait("vector", f"Bb_{bi}")
                w.wait("vector", f"dtu_{idx}")
                vop("tensor_mul", bu_t[:, 0:TPC], chunk(dtu_w, hb, TPC),
                    chunk(Bb, bi % 2, TPC),
                    label=f"bu_last_{bi}" if hb == HB - 1 else None)
                w.wait("vector", f"a_{si}")
                init = zcol[:, 0:1] if first_of_batch else \
                    carry[:, hb * NST + n:hb * NST + n + 1]

                def scn(e, si=si, init=init):
                    return e.tensor_tensor_scan(
                        out=st_t[:, 0:TPC],
                        data0=chunk(a_t, si % 2, TPC), data1=bu_t[:, 0:TPC],
                        initial=init, op0=ALU.mult, op1=ALU.add)
                w.emit("vector", scn, inc="v", label=f"scan_{si}")
                w.wait("vector", f"scan_{si}")
                vop("tensor_copy",
                    out=carry[:, hb * NST + n:hb * NST + n + 1],
                    in_=st_t[:, 0:TPC][:, TPC - 1:TPC])
                w.wait("vector", f"Cb_{bi}")
                vop("tensor_mul", yt_t[:, 0:TPC], st_t[:, 0:TPC],
                    chunk(Cb, bi % 2, TPC))
                vop("tensor_add",
                    y_w[:, hb * TPC:(hb + 1) * TPC],
                    y_w[:, hb * TPC:(hb + 1) * TPC],
                    yt_t[:, 0:TPC],
                    label=f"y_last_{bi}" if hb == HB - 1 else None)
                yield
        for hb in range(HB):
            idx = wi * HB + hb
            ysl = y_w[:, hb * TPC:(hb + 1) * TPC]
            vop("scalar_tensor_tensor", ysl, chunk(uw, idx % 2, TPC),
                dcol[:, hb:hb + 1], ysl, ALU.mult, ALU.add,
                label=f"ywf_{idx}")
            w.wait("scalar", f"ywf_{idx}")
            w.dma("scalar", aa_in[wi, hb * 128:(hb + 1) * 128, :], ysl,
                  label=f"yout_{idx}", sem="qaa")
            if p.dump:
                w.dma("scalar",
                      dumps["dmp_ymix"][hb * 128:(hb + 1) * 128,
                                        wi * TPC:(wi + 1) * TPC], ysl,
                      label=f"dym_{idx}", sem="qdmp")
                w.wait("vector", f"dym_{idx}")
            yield

    slice_ctr = [0]

    def _emit_wslice(inap, width, tag):
        gi = slice_ctr[0]
        slice_ctr[0] += 1
        if gi >= 4:
            w.wait("gpsimd", f"wused_{gi-4}")
        w.dma("gpsimd", chunk(wbf, gi % 4, 2048)[:, 0:width], inap,
              label=f"wld_{gi}", sem=f"qw{gi % 4}")
        return gi

    def emit_ffn_expert(e_):
        nmj = MJS if e_ == NE else MJ
        shared = e_ == NE
        if not shared:
            w.wait("gpsimd", "g_out")
            if e_ >= 1:
                w.wait("gpsimd", f"gbc_used_{e_-1}")
            w.dma("gpsimd", g_bc[:, 0:TPC],
                  bounce_g[e_:e_ + 1, :].broadcast_to([128, TPC]),
                  label=f"gld_{e_}", sem="qgl")
        halves = range(2) if shared else range(1)
        for half in halves:
            mjs = range(half * MJ, half * MJ + MJ) if shared else range(MJ)
            for mj in mjs:
                mloc = mj - (half * MJ if shared else 0)
                if shared:
                    gin = sg_d[mj]
                    uin = su_d[mj]
                else:
                    gin = wg_d[e_, mj]
                    uin = wu_d[e_, mj]
                gi_g = _emit_wslice(gin, DC * 128, f"g{e_}_{mj}")
                gi_u = _emit_wslice(uin, DC * 128, f"u{e_}_{mj}")
                for tt in range(NTT):
                    uidx = f"{e_}_{tt}_{mj}"
                    pg = ps[(mloc % 3) * 2]
                    pu = ps[(mloc % 3) * 2 + 1]
                    tcs = slice(tt * TT, (tt + 1) * TT)
                    if tt >= 1:
                        w.wait("tensor", f"hq_{e_}_{tt-1}_{mj}")
                    if tt == 0:
                        w.wait("tensor", f"wld_{gi_g}")
                        w.wait("tensor", f"wld_{gi_u}")
                        prev_same = {0: 9, 1: 10, 2: 8}[mloc % 3]
                        if mloc >= 3:
                            w.wait("tensor", f"hq_{e_}_{NTT-1}_{mj-3}")
                        elif half == 1:
                            w.wait("tensor", f"hq_{e_}_{NTT-1}_{prev_same}")
                        elif e_ > 0:
                            pb = MJ if e_ - 1 == NE else 0
                            w.wait("tensor",
                                   f"hq_{e_-1}_{NTT-1}_{pb + prev_same}")
                        else:
                            w.wait("tensor", "var_done")
                    for c in range(DC):
                        mm(pg[:, 0:TT],
                           chunk(wbf, gi_g % 4, 2048)[:, c * 128:(c + 1) * 128],
                           chunk(xn, c, TPC)[:, tcs],
                           start=(c == 0), stop=(c == DC - 1),
                           label=f"hg_{uidx}" if c == DC - 1 else None)
                    for c in range(DC):
                        mm(pu[:, 0:TT],
                           chunk(wbf, gi_u % 4, 2048)[:, c * 128:(c + 1) * 128],
                           chunk(xn, c, TPC)[:, tcs],
                           start=(c == 0), stop=(c == DC - 1),
                           label=f"hu_{uidx}" if c == DC - 1 else None)
                    if tt == NTT - 1:
                        w.alias_late(f"wused_{gi_g}", f"hg_{uidx}")
                        w.alias_late(f"wused_{gi_u}", f"hu_{uidx}")
                    w.wait("scalar", f"hg_{uidx}")
                    sqi = (2 * mloc + tt) % 2
                    step = 2 * mloc + tt
                    if step >= 2:
                        pmj, ptt = divmod(step - 2, 2)
                        pmjg = pmj + (half * MJ if shared else 0)
                        w.wait("scalar", f"hq_{e_}_{ptt}_{pmjg}")
                    act(chunk(s_t, sqi, TT), pg[:, 0:TT], AF.Tanh, scale=ccst[:, 2:3])
                    act(chunk(s_t, sqi, TT), chunk(s_t, sqi, TT),
                        AF.Identity, scale=ccst[:, 2:3], bias=ccst[:, 2:3],
                        label=f"sil_{uidx}")
                    w.wait("vector", f"sil_{uidx}")
                    w.wait("vector", f"hu_{uidx}")
                    if mloc == 0 and tt == 0:
                        if shared and half == 1:
                            w.wait("vector", f"dn_last_{e_}_0")
                        elif e_ > 0:
                            w.wait("vector",
                                   f"dn_last_{e_-1}_{1 if e_-1 == NE else 0}")
                    hsl = hprime[:, (mloc * 2 + tt) * TT:(mloc * 2 + tt + 1) * TT]
                    if not shared:
                        if mj == 0 and tt == 0:
                            w.wait("vector", f"gld_{e_}")
                        vop("tensor_mul", chunk(q_t, sqi, TT), pu[:, 0:TT],
                            g_bc[:, tcs])
                        vop("tensor_mul", chunk(q_t, sqi, TT), pg[:, 0:TT],
                            chunk(q_t, sqi, TT))
                        vop("tensor_mul", hsl, chunk(s_t, sqi, TT),
                            chunk(q_t, sqi, TT), label=f"hq_{uidx}")
                    else:
                        vop("tensor_mul", chunk(q_t, sqi, TT), pu[:, 0:TT],
                            chunk(s_t, sqi, TT))
                        vop("tensor_mul", hsl, pg[:, 0:TT],
                            chunk(q_t, sqi, TT), label=f"hq_{uidx}")
                    if tt == NTT - 1:
                        yield
            yield "predown"
            last_mj = half * MJ + MJ - 1 if shared else MJ - 1
            w.wait("tensor", f"hq_{e_}_{NTT-1}_{last_mj}")
            for k in range(DC):
                if shared:
                    inap = sd_d[k][:, half * MJ * 128:(half * MJ + MJ) * 128]
                else:
                    inap = wd_d[e_, k][:, half * MJ * 128:(half * MJ + MJ) * 128]
                gi = _emit_wslice(inap, MJ * 128, f"d{e_}_{k}")
                w.wait("tensor", f"wld_{gi}")
                if e_ == 0 and half == 0 and k < 2:
                    w.wait("tensor", f"dtw_{NC*HB-1}")
                for tt in range(NTT):
                    prev_k = k - (2 if NTT == 1 else 1)
                    if prev_k >= 0:
                        w.wait("tensor", f"ytd_{e_}_{half}_{tt}_{prev_k}")
                    elif e_ > 0 or half == 1:
                        pe, ph = (e_, 0) if half == 1 else \
                            (e_ - 1, 1 if e_ - 1 == NE else 0)
                        pk = DC + prev_k
                        w.wait("tensor", f"ytd_{pe}_{ph}_{tt}_{pk}")
                for j in range(MJ):
                    for tt in range(NTT):
                        pd = ps[6 + tt % 2] if NTT > 1 else ps[6 + k % 2]
                        mm(pd[:, 0:TT],
                           chunk(wbf, gi % 4, 2048)[:, j * 128:(j + 1) * 128],
                           hprime[:, (j * 2 + tt) * TT:(j * 2 + tt + 1) * TT],
                           start=(j == 0), stop=(j == MJ - 1),
                           label=(f"dnm_{e_}_{half}_{tt}_{k}"
                                  if j == MJ - 1 else None))
                for tt in range(NTT):
                    pd = ps[6 + tt % 2] if NTT > 1 else ps[6 + k % 2]
                    tcs = slice(tt * TT, (tt + 1) * TT)
                    kidx = f"{e_}_{half}_{tt}_{k}"
                    w.wait("vector", f"dnm_{kidx}")
                    vop("tensor_add", chunk(y_tot, k, TPC)[:, tcs],
                        chunk(y_tot, k, TPC)[:, tcs], pd[:, 0:TT],
                        label=f"ytd_{kidx}")
                yield
                if True:
                    w.alias_late(f"wused_{gi}", f"dnm_{e_}_{half}_{NTT-1}_{k}")
            last = f"ytd_{e_}_{half}_{NTT-1}_{DC-1}"
            w.alias_late(f"dn_last_{e_}_{half}", last)
        if not shared:
            vop("engine_nop", label=f"gbc_used_{e_}")


    def gchain(gens):
        for g in gens:
            yield from g

    pgen = emit_p34()
    sgen = gchain([emit_scan_window(wi) for wi in range(NC)]) \
        if p.phase in ("all", "scan") else iter(())
    fgen = gchain([emit_ffn_expert(e_) for e_ in range(NE + 1)]) \
        if p.phase in ("all", "ffn") else iter(())
    if p.interleave:
        p_live = True
        f_hold = False
        while p_live:
            try:
                next(pgen)
            except StopIteration:
                p_live = False
            for _ in range(2):
                if f_hold:
                    break
                try:
                    if next(fgen) == "predown":
                        f_hold = True
                except StopIteration:
                    break
        F_TOT = (NE * (MJ + DC) + (MJS + 2 * DC)) or 1
        S_TOT = (NC * (2 + 2 * NST + 2)) or 1
        f_live = s_live = True
        fd = sd = 0
        while f_live or s_live:
            if s_live and (not f_live or fd * S_TOT >= sd * F_TOT):
                try:
                    next(sgen)
                    sd += 1
                except StopIteration:
                    s_live = False
            elif f_live:
                try:
                    next(fgen)
                    fd += 1
                except StopIteration:
                    f_live = False
    else:
        for _ in pgen:
            pass
        for _ in sgen:
            pass
        for _ in fgen:
            pass
    w.resolve_late()

    if p.phase in ("all", "scan"):
        for wi in range(NC):
            for hb in range(HB):
                w.wait("gpsimd", f"yout_{wi * HB + hb}")
        coll("AllToAll", ALU.bypass, [aa_in.opt()], [aa_out.opt()], "cc_aa")

    for c in range(DC):
        if c >= 2:
            w.wait("scalar", f"fin_{c-2}")
        if c == 0 and p.phase in ("all", "scan"):
            w.wait("scalar", f"ywf_{NC*HB-1}")
        w.dma("scalar", chunk(xfin, c % 2, TPC),
              x_own_tok[c * 128:(c + 1) * 128, :], label=f"xf_{c}",
              sem=f"qxf{c % 2}")
        if p.phase in ("all", "scan"):
            w.wait("scalar", "cc_aa")
            w.dma("scalar", chunk(ymr, c % 2, TPC),
                  aa_out[(c * 128) // CH,
                         (c * 128) % CH:(c * 128) % CH + 128, :],
                  label=f"fload_{c}", sem=f"qfl{c % 2}")
            w.wait("vector", f"fload_{c}")
        w.wait("vector", f"xf_{c}")
        if c >= 2:
            w.wait("vector", f"fin_{c-2}")
        if p.phase in ("all", "ffn"):
            w.wait("vector", f"ytd_{NE}_1_{NTT-1}_{c}")
        if p.phase == "all":
            vop("tensor_add", sq, chunk(ymr, c % 2, TPC),
                chunk(y_tot, c, TPC))
        elif p.phase == "scan":
            vop("tensor_copy", out=sq, in_=chunk(ymr, c % 2, TPC))
        elif p.phase == "ffn":
            vop("tensor_copy", out=sq, in_=chunk(y_tot, c, TPC))
        else:
            vop("memset", sq, 0.0)
        vop("tensor_add", chunk(ofin, c % 2, TPC), sq,
            chunk(xfin, c % 2, TPC), label=f"fadd_{c}")
        w.wait("scalar", f"fadd_{c}")
        w.dma("scalar", out_t[c * 128:(c + 1) * 128, :],
              chunk(ofin, c % 2, TPC), label=f"fin_{c}", sem=f"qout{c % 2}")
    w.wait("sync", f"fin_{DC-1}")

    def _final(e):
        for name, sem in w.sems.items():
            if name.startswith("q") and w.count[name] > 0:
                e.wait_ge(sem, w.count[name])
    w.plan["sync"].append(_final)

    with nc.Block() as block:
        @block.sync
        def _(eng):
            for f in w.plan["sync"]:
                f(eng)

        @block.scalar
        def _(eng):
            for f in w.plan["scalar"]:
                f(eng)

        @block.vector
        def _(eng):
            for f in w.plan["vector"]:
                f(eng)

        @block.tensor
        def _(eng):
            for f in w.plan["tensor"]:
                f(eng)

        @block.gpsimd
        def _(eng):
            for f in w.plan["gpsimd"]:
                f(eng)

    return nc


def _prep_inputs(p: P, inputs):
    x = np.ascontiguousarray(inputs["L0_attn_input"], dtype=np.float32)
    S, TPC = p.S, p.TPC
    xT = np.ascontiguousarray(x.reshape(S, D).T)
    wprojT = np.ascontiguousarray(
        np.concatenate([np.asarray(inputs["L0_attn_W_dt_in"]).T,
                        np.asarray(inputs["L0_attn_W_B"]).T,
                        np.asarray(inputs["L0_attn_W_C"]).T],
                       axis=1).astype(np.float32))
    routerT = np.ascontiguousarray(
        np.asarray(inputs["L0_ffn_router_w"]).T.astype(np.float32))
    wdt2T = np.ascontiguousarray(
        np.asarray(inputs["L0_attn_W_dt_out"]).T.astype(np.float32))
    eye128 = np.eye(128, dtype=np.float32)
    def tile_up(wm):
        h = wm.shape[1]
        return np.ascontiguousarray(
            np.asarray(wm, np.float32).reshape(DC, 128, h // 128, 128)
            .transpose(2, 1, 0, 3).reshape(h // 128, 128, DC * 128))

    def tile_dn(wm):
        h = wm.shape[0]
        return np.ascontiguousarray(
            np.asarray(wm, np.float32).reshape(h // 128, 128, DC, 128)
            .transpose(2, 1, 0, 3).reshape(DC, 128, h))

    wg_in = np.asarray(inputs["L0_ffn_wg"], np.float32)
    wu_in = np.asarray(inputs["L0_ffn_wu"], np.float32)
    wd_in = np.asarray(inputs["L0_ffn_wd"], np.float32)
    shared = dict(
        xT=xT, wprojT=wprojT, routerT=routerT, eye128=eye128,
        wg=np.stack([tile_up(wg_in[e]) for e in range(NE)]),
        wu=np.stack([tile_up(wu_in[e]) for e in range(NE)]),
        wd=np.stack([tile_dn(wd_in[e]) for e in range(NE)]),
        sg=tile_up(np.asarray(inputs["L0_ffn_sg"], np.float32)),
        su=tile_up(np.asarray(inputs["L0_ffn_su"], np.float32)),
        sd=tile_dn(np.asarray(inputs["L0_ffn_sd"], np.float32)),
    )
    alog = np.asarray(inputs["L0_attn_A_log"], np.float32)
    dv = np.asarray(inputs["L0_attn_D"], np.float32)
    bdt = np.asarray(inputs["L0_attn_b_dt_out"], np.float32)
    in_maps = []
    for c in range(NC):
        m = dict(shared)
        m["x_own_tok"] = np.ascontiguousarray(xT[:, c * TPC:(c + 1) * TPC])
        m["x_own_ch"] = np.ascontiguousarray(xT[c * CH:(c + 1) * CH, :])
        al = alog[c * CH:(c + 1) * CH].reshape(HB, 128, NST)
        m["alog_own"] = np.ascontiguousarray(
            al.transpose(1, 0, 2).reshape(128, HB * NST))
        m["d_own"] = np.ascontiguousarray(
            dv[c * CH:(c + 1) * CH].reshape(HB, 128).T)
        m["bdt_own"] = np.ascontiguousarray(
            bdt[c * CH:(c + 1) * CH].reshape(HB, 128).T)
        m["wdt2T_own"] = np.ascontiguousarray(wdt2T[:, c * CH:(c + 1) * CH])
        in_maps.append(m)
    return in_maps


def kernel(**inputs):
    p = P()
    in_maps = _prep_inputs(p, inputs)
    nc = build(p)
    res = run_bass_kernel_spmd(nc, in_maps, core_ids=list(range(NC)))
    outT = np.concatenate([res.results[c]["out"] for c in range(NC)], axis=1)
    return np.ascontiguousarray(outT.T).reshape(2, p.S // 2, D)


# version 08 (final)
# speedup vs baseline: 1.2148x; 1.0788x over previous
import numpy as np
from dataclasses import dataclass
from contextlib import ExitStack

import concourse.bass as bass
import concourse.mybir as mybir
from concourse.bass_utils import run_bass_kernel_spmd
from concourse.aot_env import allow_ici_all_to_all

F32 = mybir.dt.float32
BF16 = mybir.dt.bfloat16
AF = mybir.ActivationFunctionType
ALU = mybir.AluOpType

NC = 8
D = 2048
DC = 16
NST = 16
RK = 64
NE = 8
HID = 1408
MJ = 11
HS = 2816
MJS = 22
CH = 256
HB = 2
EPS = 1e-5


@dataclass
class P:
    TPC: int = 1024
    dump: bool = False
    interleave: bool = True
    phase: str = "all"
    cut: int = 99

    @property
    def S(self):
        return NC * self.TPC

    @property
    def TT(self):
        return min(512, self.TPC)

    @property
    def NTT(self):
        return max(1, self.TPC // self.TT)


class Wk:
    ENGS = ("sync", "scalar", "vector", "tensor", "gpsimd")

    def __init__(self, nc, ctx):
        self.nc = nc
        self.ctx = ctx
        self.plan = {e: [] for e in self.ENGS}
        self.sems = {}
        self.count = {}
        self.labels = {}
        for name in ("v", "s", "t", "g", "cc"):
            self._mksem(name)

    def _mksem(self, name):
        if name not in self.sems:
            self.sems[name] = self.ctx.enter_context(self.nc.semaphore(name))
            self.count[name] = 0
        return self.sems[name]

    def emit(self, eng, fn, inc=None, amt=1, label=None):
        if inc is not None:
            self.count[inc] += amt
            if label is not None:
                self.labels[label] = (inc, self.count[inc])
            sem = self.sems[inc]

            def wrapped(e, fn=fn, sem=sem, amt=amt):
                fn(e).then_inc(sem, amt)
            self.plan[eng].append(wrapped)
        else:
            assert label is None
            self.plan[eng].append(fn)

    def alias(self, new, old):
        self.labels[new] = self.labels[old]

    def alias_late(self, new, old):
        self._late = getattr(self, "_late", [])
        self._late.append((new, old))

    def resolve_late(self):
        for new, old in getattr(self, "_late", []):
            self.labels[new] = self.labels[old]

    def wait(self, eng, label):
        def wrapped(e, label=label):
            name, val = self.labels[label]
            e.wait_ge(self.sems[name], val)
        self.plan[eng].append(wrapped)

    def dma(self, queue_eng, out, in_, label=None, sem="q"):
        self._mksem(sem)
        if sem == "qdmp" and self.count[sem] > 0:
            val = self.count[sem]

            def wr(e, val=val):
                e.wait_ge(self.sems["qdmp"], val)
            self.plan[queue_eng].append(wr)
        self.emit(queue_eng, lambda e, o=out, i=in_: e.dma_start(out=o, in_=i),
                  inc=sem, amt=16, label=label)


def build(p: P):
    nc = bass.Bass("TRN2", target_bir_lowering=False,
                   detect_race_conditions=False)
    TPC, S, TT, NTT = p.TPC, p.S, p.TT, p.NTT
    TSUB = TPC // 128
    NSP = (TPC + 511) // 512
    ctx = ExitStack()
    nc._build_ctx = ctx

    def din(name, shape):
        return nc.dram_tensor(name, shape, F32, kind="ExternalInput").ap()

    xT = din("xT", [D, S])
    x_own_tok = din("x_own_tok", [D, TPC])
    x_own_ch = din("x_own_ch", [CH, S])
    alog_own = din("alog_own", [128, HB * NST])
    d_own = din("d_own", [128, HB])
    bdt_own = din("bdt_own", [128, HB])
    wdt2T_own = din("wdt2T_own", [RK, CH])
    wprojT = din("wprojT", [D, RK + 2 * NST])
    routerT = din("routerT", [D, NE])
    wg_d = din("wg", [NE, MJ, 128, DC * 128])
    wu_d = din("wu", [NE, MJ, 128, DC * 128])
    wd_d = din("wd", [NE, DC, 128, MJ * 128])
    sg_d = din("sg", [MJS, 128, DC * 128])
    su_d = din("su", [MJS, 128, DC * 128])
    sd_d = din("sd", [DC, 128, MJS * 128])
    eye = din("eye128", [128, 128])

    out_t = nc.dram_tensor("out", [D, TPC], F32, kind="ExternalOutput").ap()
    dumps = {}
    if p.dump:
        for nm, sh, ddt in [("dmp_stats", [2, TPC], F32),
                            ("dmp_dtr", [RK, S], F32),
                            ("dmp_bc", [2 * NST, S], F32),
                            ("dmp_u", [CH, S], F32),
                            ("dmp_dt", [CH, S], BF16),
                            ("dmp_ymix", [CH, S], BF16),
                            ("dmp_gates", [NE, TPC], F32),
                            ("dmp_xn", [D, TPC], F32)]:
            dumps[nm] = nc.dram_tensor(nm, sh, ddt, kind="ExternalOutput").ap()

    stats_mine = nc.dram_tensor("stats_mine", [2, TPC], F32).ap()
    stats_all = nc.dram_tensor("stats_all", [NC, 2, TPC], F32).ap()
    bounce_bc = nc.dram_tensor("bounce_bc", [2 * NST, S], BF16).ap()
    bounce_g = nc.dram_tensor("bounce_g", [NE, TPC], F32).ap()
    bounce_u = nc.dram_tensor("bounce_u", [CH, S], BF16).ap()
    bounce_dtr = nc.dram_tensor("bounce_dtr", [RK, S], BF16).ap()
    bounce_dt = nc.dram_tensor("bounce_dt", [CH, S], BF16).ap()
    aa_in = nc.dram_tensor("aa_in", [NC, CH, TPC], BF16).ap()
    aa_out = nc.dram_tensor("aa_out", [NC, CH, TPC], BF16).ap()

    sb = {}

    def t(name, shape, dt=F32):
        sb[name] = ctx.enter_context(nc.sbuf_tensor(name, shape, dt))
        return sb[name]

    xn = t("xn", [128, DC * TPC], BF16)
    y_tot = t("y_tot", [128, DC * TPC], BF16)
    dtrp = t("dtrp", [RK, TPC], BF16)
    dt_sc = t("dt_sc", [128, 2 * TPC], BF16)
    t("dtrw", [RK, 2 * TPC], BF16)
    cstage = t("cstage", [128, 2048], F32)
    wprojb = t("wprojb", [128, DC * 96], BF16)
    routb = t("routb", [128, DC * NE], BF16)
    wdt2b = t("wdt2b", [RK, CH], BF16)
    aneg = t("aneg", [128, HB * NST], F32)
    dcol = t("dcol", [128, HB], F32)
    bcol = t("bcol", [128, HB], F32)
    wsum_neg = t("wsum_neg", [96, 1], F32)
    eye_sb = t("eye_sb", [128, 128], F32)
    ones_b = t("ones_b", [128, 1], BF16)
    ones_f = t("ones_f", [128, 1], F32)
    zcol = t("zcol", [128, 1], F32)
    carry = t("carry", [128, HB * NST], F32)
    ccst = t("ccst", [128, 4], F32)

    gtmp = t("gtmp", [128, 6 * NE], F32)
    gates_fm = t("gates_fm", [NE, TPC], F32)

    xo = t("xo", [128, 2 * TPC], F32)
    xpieceb = t("xpieceb", [128, 4 * TPC], BF16)
    ar1 = t("ar1", [128, 4 * TPC], F32)
    mb = ar1[:, 0:TPC]
    rsb = ar1[:, TPC:2 * TPC]
    sq = ar1[:, 2 * TPC:3 * TPC]
    fx = ar1[0:96, 3 * TPC:4 * TPC]
    bc_piece = t("bc_piece", [2 * NST, TPC], BF16)

    a_t = cstage if 2 * TPC <= 2048 else t("a_t", [128, 2 * TPC], F32)
    bu_t = t("bu_t", [128, TPC], F32)
    yt_t = t("yt_t", [128, TPC], BF16)
    mrow = bu_t[0:1, 0:TPC]
    rsrow = cstage[0:1, TPC:2 * TPC]
    srow = cstage[0:1, 0:TPC]
    st_t = t("st_t", [128, TPC], BF16)
    Bb = t("Bb", [128, 2 * TPC], BF16)
    Cb = t("Cb", [128, 2 * TPC], BF16)
    spl = t("spl", [128, TPC], BF16)
    dtu_w = t("dtu_w", [128, HB * TPC], BF16)
    uw = t("uw", [128, 2 * TPC], BF16)
    y_w = t("y_w", [128, HB * TPC], BF16)
    dtp = y_w

    wbf = t("wbf", [128, 4 * 2048], BF16)
    hprime = t("hprime", [128, MJ * 2 * TT], BF16)
    s_t = t("s_t", [128, 2 * TT], BF16)
    q_t = t("q_t", [128, 2 * TT], BF16)
    g_bc = t("g_bc", [128, TPC], BF16)

    ymr = xpieceb
    xfin = xo
    ofin = ar1[:, 0:2 * TPC]

    ps = [ctx.enter_context(nc.psum_tensor(f"ps{i}", [128, 512], F32))
          for i in range(8)]

    w = Wk(nc, ctx)

    def act(out, in_, func, bias=0.0, scale=1.0, label=None):
        def fn(e, out=out, in_=in_, func=func, bias=bias, scale=scale):
            return e.activation(out=out, in_=in_, func=func, bias=bias,
                                scale=scale)
        w.emit("scalar", fn, inc="s" if label else None, label=label)

    _vs = [0]

    def vop(name, *args, eng="vector", label=None, **kw):
        def fn(e, name=name, args=args, kw=kw):
            return getattr(e, name)(*args, **kw)
        w.emit(eng, fn, inc=("v" if eng == "vector" else "g") if label else None,
               label=label)

    def vsync(name, *args, **kw):
        _vs[0] += 1
        lbl = f"_vs{_vs[0]}"
        vop(name, *args, label=lbl, **kw)
        w.wait("vector", lbl)

    def mm(out, lhsT, rhs, start, stop, label=None):
        def fn(e, out=out, lhsT=lhsT, rhs=rhs, start=start, stop=stop):
            return e.matmul(out, lhsT, rhs, start=start, stop=stop)
        w.emit("tensor", fn, inc="t" if label else None, label=label)

    def chunk(tile, i, width, rows=128):
        return tile[0:rows, i * width:(i + 1) * width]

    w.dma("sync", cstage[:, 0:DC * 96],
          wprojT.rearrange("(c p) k -> p c k", p=128), sem="qc")
    w.dma("sync", cstage[:, DC * 96:DC * 96 + DC * NE],
          routerT.rearrange("(c p) k -> p c k", p=128), sem="qc")
    w.dma("sync", cstage[0:128, 1664:1664 + HB * NST], alog_own, sem="qc")
    w.dma("sync", cstage[0:RK, 1696:1696 + CH], wdt2T_own, sem="qc")
    w.dma("sync", eye_sb[:], eye, sem="qc")
    w.dma("sync", dcol[:], d_own, sem="qc")
    w.dma("sync", bcol[:], bdt_own, label="ld_const", sem="qc")

    w.wait("vector", "ld_const")
    vop("memset", ccst[:, 0:1], EPS)
    vop("memset", ccst[:, 1:2], -0.5)
    vop("memset", ccst[:, 2:3], 0.5)
    vop("memset", ones_b[:], 1.0)
    vop("memset", ones_f[:], 1.0)
    vop("memset", zcol[:], 0.0)
    vop("tensor_copy", out=wprojb[:], in_=cstage[:, 0:DC * 96])
    vop("tensor_copy", out=routb[:], in_=cstage[:, DC * 96:DC * 96 + DC * NE])
    vop("tensor_copy", out=wdt2b[:], in_=cstage[0:RK, 1696:1696 + CH])
    for i in range(DC):
        vop("memset", chunk(y_tot, i, TPC), 0.0)
    vop("engine_nop", label="consts_cast")
    w.wait("scalar", "ld_const")
    act(aneg[:], cstage[0:128, 1664:1664 + HB * NST], AF.Exp, label="an1")
    w.wait("vector", "an1")
    vop("tensor_scalar_mul", aneg[:], aneg[:], -1.0, label="aneg_done")

    w.wait("tensor", "consts_cast")
    for c in range(DC):
        mm(ps[6][0:96, 0:1], chunk(wprojb, c, 96), ones_b[:],
           start=(c == 0), stop=(c == DC - 1),
           label="wsum_mm" if c == DC - 1 else None)
    w.wait("vector", "wsum_mm")
    vop("tensor_scalar_mul", wsum_neg[:], ps[6][0:96, 0:1], -1.0,
        label="wsum_done")

    for i in range(DC) if p.cut >= 1 else []:
        if i >= 2:
            w.wait("scalar", f"stat_mm_{i-2}")
        w.dma("scalar", chunk(xo, i % 2, TPC),
              x_own_tok[i * 128:(i + 1) * 128, :], label=f"xo_{i}",
              sem=f"qxo{i % 2}")
        w.wait("scalar", f"xo_{i}")
        if i >= 1:
            w.wait("scalar", f"stat_mm_{i-1}")
        act(sq, chunk(xo, i % 2, TPC), AF.Square, label=f"sqr_{i}")
        if i == 0:
            w.wait("tensor", "wsum_done")
        w.wait("tensor", f"sqr_{i}")
        for h in range(NSP):
            cw = min(512, TPC - h * 512)
            mm(ps[4 + h][0:1, 0:cw], ones_f[:],
               chunk(xo, i % 2, TPC)[:, h * 512:h * 512 + cw],
               start=(i == 0), stop=(i == DC - 1))
            mm(ps[6 + h][0:1, 0:cw], ones_f[:], sq[:, h * 512:h * 512 + cw],
               start=(i == 0), stop=(i == DC - 1),
               label=f"stat_mm_{i}" if h == NSP - 1 else None)
    if p.cut >= 1:
      w.wait("vector", f"stat_mm_{DC-1}")
    for h in range(NSP):
        cw = min(512, TPC - h * 512)
        vop("tensor_scalar_mul", mrow[:, h * 512:h * 512 + cw],
            ps[4 + h][0:1, 0:cw], 1.0 / D)
        vop("tensor_scalar_mul", srow[:, h * 512:h * 512 + cw],
            ps[6 + h][0:1, 0:cw], 1.0 / D,
            label="stats_evac" if h == NSP - 1 else None)
    vop("tensor_mul", sq[0:1, 0:TPC], mrow, mrow)
    vop("tensor_sub", srow, srow, sq[0:1, 0:TPC],
        label="var_done")
    w.wait("scalar", "var_done")
    act(srow, srow, AF.Ln, bias=ccst[0:1, 0:1],
        label="lnv")
    act(rsrow, srow, AF.Exp, scale=ccst[0:1, 1:2],
        label="rs_done")
    w.wait("scalar", "rs_done")
    w.dma("scalar", stats_mine[0:1, :], mrow, sem="qst")
    w.dma("scalar", stats_mine[1:2, :], rsrow, label="stats_out", sem="qst")
    if p.dump:
        w.dma("scalar", dumps["dmp_stats"][0:1, :], mrow, sem="qdmp")
        w.dma("scalar", dumps["dmp_stats"][1:2, :], rsrow, label="dstat",
              sem="qdmp")

    def coll(kind, op, ins, outs, label):
        def fn(e, kind=kind, op=op, ins=ins, outs=outs):
            if kind == "AllToAll":
                with allow_ici_all_to_all():
                    return e.collective_compute(
                        kind, op, replica_groups=[list(range(NC))],
                        ins=ins, outs=outs)
            return e.collective_compute(
                kind, op, replica_groups=[list(range(NC))], ins=ins, outs=outs)
        w.emit("gpsimd", fn, inc="cc", label=label)

    if p.cut >= 2:
        w.wait("gpsimd", "stats_out")
        coll("AllGather", ALU.bypass, [stats_mine.opt()], [stats_all.opt()],
             "cc_stats")

    for i in range(DC) if p.cut >= 3 else []:
        if i >= 2:
            w.wait("scalar", f"xn_{i-2}")
        w.dma("scalar", chunk(xo, i % 2, TPC),
              x_own_tok[i * 128:(i + 1) * 128, :], label=f"xo2_{i}",
              sem=f"qxo{i % 2}")
        if i == 0:
            w.wait("scalar", "rs_done")
            w.dma("scalar", mb,
                  stats_mine[0:1, :].broadcast_to([128, TPC]), sem="qmr")
            w.dma("scalar", rsb,
                  stats_mine[1:2, :].broadcast_to([128, TPC]), label="ownbc",
                  sem="qmr")
        w.wait("vector", f"xo2_{i}")
        if i == 0:
            w.wait("vector", "ownbc")
        vop("tensor_sub", sq, chunk(xo, i % 2, TPC), mb)
        vop("tensor_mul", chunk(xn, i, TPC), sq, rsb,
            label=f"xn_{i}")

    w.wait("tensor", f"xn_{DC-1}") if p.cut >= 31 else None
    w.wait("tensor", "var_done") if p.cut >= 31 else None
    for ts in range(TSUB) if p.cut >= 31 else []:
        if ts >= 1:
            w.wait("tensor", f"rlec_{ts-1}")
        for c in range(DC):
            mm(ps[6][:, ts * NE:(ts + 1) * NE],
               chunk(xn, c, TPC)[:, ts * 128:(ts + 1) * 128],
               chunk(routb, c, NE), start=(c == 0), stop=(c == DC - 1),
               label=f"rl_{ts}" if c == DC - 1 else None)
        w.wait("vector", f"rl_{ts}")
        L = gtmp[:, 0:NE]
        vop("tensor_copy", out=L, in_=ps[6][:, ts * NE:(ts + 1) * NE],
            label=f"rlec_{ts}")
        w.wait("vector", f"rlec_{ts}")
        if p.cut == 31:
            w.alias(f"gw_{ts}", f"rlec_{ts}")
            continue
        mx = gtmp[:, NE:NE + 1]
        vsync("reduce_max", mx, L, axis=mybir.AxisListType.X)
        vop("tensor_scalar_mul", mx, mx, -1.0, label=f"gmx_{ts}")
        w.wait("vector", f"gmx_{ts}")
        w.wait("scalar", f"gmx_{ts}")
        act(L, L, AF.Exp, bias=mx, label=f"gex_{ts}")
        w.wait("vector", f"gex_{ts}")
        m1 = gtmp[:, NE + 1:NE + 2]
        vsync("reduce_max", m1, L, axis=mybir.AxisListType.X)
        msk = gtmp[:, 2 * NE:3 * NE]
        vsync("tensor_scalar", msk, L, m1, None, ALU.is_ge)
        vsync("tensor_scalar", msk, msk, -1.0, 1.0, ALU.mult, ALU.add)
        g2 = gtmp[:, 3 * NE:4 * NE]
        vsync("tensor_mul", g2, L, msk)
        m2 = gtmp[:, NE + 2:NE + 3]
        vsync("reduce_max", m2, g2, axis=mybir.AxisListType.X)
        vsync("tensor_scalar", msk, L, m2, None, ALU.is_ge)
        den = gtmp[:, NE + 3:NE + 4]
        vsync("tensor_add", den, m1, m2)
        rec = gtmp[:, NE + 4:NE + 5]
        vsync("reciprocal", rec, den)
        vsync("tensor_mul", L, L, msk)
        wslot = gtmp[:, (4 + (ts % 2)) * NE:(5 + (ts % 2)) * NE]
        vop("tensor_scalar_mul", wslot, L, rec, label=f"gw_{ts}")
        w.wait("vector", f"gw_{ts}")
        if p.cut == 32:
            continue
        w.wait("tensor", f"gw_{ts}")
        if ts >= 1:
            w.wait("tensor", f"gfm_{ts-1}")

        def tp(e, wslot=wslot):
            return e.transpose(ps[7][0:NE, 0:128], wslot, eye_sb[:])
        w.emit("tensor", tp, inc="t", label=f"gt_{ts}")
        w.wait("vector", f"gt_{ts}")
        vop("tensor_copy", out=gates_fm[:, ts * 128:(ts + 1) * 128],
            in_=ps[7][0:NE, 0:128], label=f"gfm_{ts}")
    if p.cut >= 34:
        w.wait("scalar", f"gfm_{TSUB-1}")
        w.dma("scalar", bounce_g[:, :], gates_fm[:, 0:TPC], label="g_out",
              sem="qgo")
    if p.dump and p.cut >= 34:
        w.dma("scalar", dumps["dmp_gates"], gates_fm[:, 0:TPC], sem="qdmp")
        for i in range(DC):
            w.wait("vector", f"xn_{i}")
            vop("tensor_copy", out=sq, in_=chunk(xn, i, TPC),
                label=f"dxn_{i}")
            w.wait("scalar", f"dxn_{i}")
            w.dma("scalar", dumps["dmp_xn"][i * 128:(i + 1) * 128, :],
                  sq, label=f"dxnd_{i}", sem="qdmp")
            w.wait("vector", f"dxnd_{i}")

    def emit_p34():
      for g in range(NC) if p.cut >= 40 else []:
          w.wait("scalar", "cc_stats")
          if g == 0:
              w.wait("scalar", f"xn_{DC-1}")
          else:
              w.wait("scalar", f"fix_{g-1}")
          w.dma("scalar", mb,
                stats_all[g, 0:1, :].broadcast_to([128, TPC]), sem="qmr")
          w.dma("scalar", rsb,
                stats_all[g, 1:2, :].broadcast_to([128, TPC]), label=f"gbc_{g}",
                sem="qmr")
          for hb in range(HB):
              idx = g * HB + hb
              if idx >= 2:
                  w.wait("scalar", f"u_{idx-2}")
              elif g == 0:
                  w.wait("scalar", f"xn_{DC-1}")
              w.dma("scalar", chunk(xo, idx % 2, TPC),
                    x_own_ch[hb * 128:(hb + 1) * 128, g * TPC:(g + 1) * TPC],
                    label=f"xch_{idx}", sem=f"qxo{idx % 2}")
              w.wait("vector", f"xch_{idx}")
              if hb == 0:
                  w.wait("vector", f"gbc_{g}")
              if idx >= 2:
                  w.wait("vector", f"ubo_{idx-2}")
              vop("tensor_sub", sq, chunk(xo, idx % 2, TPC),
                  mb)
              vop("tensor_mul", chunk(uw, idx % 2, TPC), sq,
                  rsb, label=f"u_{idx}")
              w.wait("scalar", f"u_{idx}")
              w.dma("scalar",
                    bounce_u[hb * 128:(hb + 1) * 128, g * TPC:(g + 1) * TPC],
                    chunk(uw, idx % 2, TPC), label=f"ubo_{idx}",
                    sem=f"qub{idx % 2}")
              if p.dump:
                  vop("tensor_copy", out=sq,
                      in_=chunk(uw, idx % 2, TPC), label=f"du_{idx}")
                  w.wait("scalar", f"du_{idx}")
                  w.dma("scalar",
                        dumps["dmp_u"][hb * 128:(hb + 1) * 128,
                                       g * TPC:(g + 1) * TPC],
                        sq, label=f"dud_{idx}", sem="qdmp")
                  w.wait("vector", f"dud_{idx}")
          for c in range(DC):
              idx = g * DC + c
              if idx >= 4:
                  w.wait("gpsimd", f"pmm_{idx-4}")
              w.dma("gpsimd", chunk(xpieceb, idx % 4, TPC),
                    xT[c * 128:(c + 1) * 128, g * TPC:(g + 1) * TPC],
                    label=f"xp_{idx}", sem=f"qxp{idx % 4}")
              w.wait("tensor", f"xp_{idx}")
              if c == 0:
                  w.wait("tensor", "wsum_done")
                  w.wait("tensor", f"gfm_{TSUB-1}")
                  if g >= 1:
                      w.wait("tensor", f"fix_{g-1}")
              for h in range(NSP):
                  cw = min(512, TPC - h * 512)
                  mm(ps[6 + h][0:96, 0:cw], chunk(wprojb, c, 96),
                     chunk(xpieceb, idx % 4, TPC)[:, h * 512:h * 512 + cw],
                     start=(c == 0), stop=(c == DC - 1),
                     label=f"pmm_{idx}" if h == NSP - 1 else None)
              if c == DC - 1:
                  w.alias(f"prj_{g}", f"pmm_{idx}")
          w.wait("vector", f"prj_{g}")
          for h in range(NSP):
              cw = min(512, TPC - h * 512)
              cs = slice(h * 512, h * 512 + cw)
              vop("scalar_tensor_tensor", fx[0:96, cs], mb[0:96, cs],
                  wsum_neg[0:96, 0:1], ps[6 + h][0:96, 0:cw], ALU.mult, ALU.add)
          vop("tensor_mul", fx[0:96, 0:TPC], fx[0:96, 0:TPC], rsb[0:96, :])
          if g >= 1:
              w.wait("vector", f"dtrb_{g-1}")
              w.wait("vector", f"bcb_{g-1}")
          vop("tensor_copy", out=dtrp[:, 0:TPC], in_=fx[0:RK, 0:TPC])
          vop("tensor_copy", out=bc_piece[:, 0:TPC], in_=fx[RK:96, 0:TPC],
              label=f"fix_{g}")
          w.wait("scalar", f"fix_{g}")
          w.dma("scalar", bounce_dtr[:, g * TPC:(g + 1) * TPC],
                dtrp[:, 0:TPC], label=f"dtrb_{g}", sem=f"qdtr{g % 2}")
          w.dma("scalar", bounce_bc[:, g * TPC:(g + 1) * TPC],
                bc_piece[:, 0:TPC], label=f"bcb_{g}", sem=f"qbcb{g % 2}")
          yield
          if p.dump:
              w.dma("scalar", dumps["dmp_dtr"][:, g * TPC:(g + 1) * TPC],
                    fx[0:RK, 0:TPC], sem="qdmp")
              w.dma("scalar", dumps["dmp_bc"][:, g * TPC:(g + 1) * TPC],
                    fx[RK:96, 0:TPC], label=f"dbc_{g}", sem="qdmp")
              w.wait("vector", f"dbc_{g}")


      for wi in range(NC) if p.cut >= 40 else []:
          w.wait("scalar", f"dtrb_{NC-2}")
          w.wait("scalar", f"dtrb_{NC-1}")
          if wi >= 2:
              w.wait("scalar", f"dt2_{(wi - 2) * HB + HB - 1}")
          w.dma("scalar", dtrp[:, 0:TPC] if False else sb["dtrw"][0:RK, (wi % 2) * TPC:(wi % 2) * TPC + TPC],
                bounce_dtr[:, wi * TPC:(wi + 1) * TPC], label=f"dtrl_{wi}",
                sem=f"qdl{wi % 2}")
          for hb in range(HB):
              idx = wi * HB + hb
              if idx == 0:
                  w.wait("tensor", f"fix_{NC-1}")
              else:
                  w.wait("tensor", f"dtw_{idx-1}")
              w.wait("tensor", f"dtrl_{wi}")
              for h in range(NSP):
                  cw = min(512, TPC - h * 512)
                  mm(ps[6 + h][:, 0:cw], wdt2b[0:RK, hb * 128:(hb + 1) * 128],
                     sb["dtrw"][0:RK, (wi % 2) * TPC + h * 512:
                                (wi % 2) * TPC + h * 512 + cw],
                     start=True, stop=True,
                     label=f"dt2_{idx}" if h == NSP - 1 else None)
              w.wait("scalar", f"dt2_{idx}")
              if idx >= 2:
                  w.wait("scalar", f"dtbo_{idx-2}")
              dsl = chunk(dtp, idx % 2, TPC)
              for h in range(NSP):
                  cw = min(512, TPC - h * 512)
                  cs = slice(h * 512, h * 512 + cw)
                  act(spl[:, cs], ps[6 + h][:, 0:cw], AF.Abs,
                      bias=bcol[:, hb:hb + 1])
                  act(spl[:, cs], spl[:, cs], AF.Exp, scale=-1.0)
                  act(spl[:, cs], spl[:, cs], AF.Ln, bias=1.0)
                  act(dsl[:, cs], ps[6 + h][:, 0:cw],
                      AF.Relu, bias=bcol[:, hb:hb + 1],
                      label=f"spx_{idx}" if h == NSP - 1 else None)
              w.wait("vector", f"spx_{idx}")
              vop("tensor_add", dsl, dsl, spl[:, 0:TPC], label=f"dtw_{idx}")
              w.wait("scalar", f"dtw_{idx}")
              w.dma("scalar", bounce_dt[hb * 128:(hb + 1) * 128,
                                        wi * TPC:(wi + 1) * TPC],
                    dsl, label=f"dtbo_{idx}", sem=f"qdt{idx % 2}")
              if p.dump:
                  w.dma("scalar", dumps["dmp_dt"][hb * 128:(hb + 1) * 128,
                                                  wi * TPC:(wi + 1) * TPC],
                        dsl, label=f"ddtd_{idx}", sem="qdmp")
                  w.wait("vector", f"ddtd_{idx}")
              yield

    n_scan = [0]

    def emit_scan_window(wi):
        first_of_batch = (wi % (NC // 2) == 0)
        for hb in range(HB):
            idx = wi * HB + hb
            if wi >= 2:
                w.wait("scalar", f"ywf_{(wi-2)*HB+hb}")
            w.wait("scalar", f"ubo_{NC*HB-2}")
            w.wait("scalar", f"ubo_{NC*HB-1}")
            w.dma("scalar", chunk(uw, idx % 2, TPC),
                  bounce_u[hb * 128:(hb + 1) * 128, wi * TPC:(wi + 1) * TPC],
                  label=f"uwl_{idx}", sem=f"quw{idx % 2}")
            w.wait("scalar", f"dtbo_{NC*HB-2}")
            w.wait("scalar", f"dtbo_{NC*HB-1}")
            if idx >= 2:
                w.wait("scalar", f"a_last_{idx-2}")
                w.wait("scalar", f"dtu_{idx-2}")
            w.dma("scalar", chunk(dt_sc, idx % 2, TPC),
                  bounce_dt[hb * 128:(hb + 1) * 128, wi * TPC:(wi + 1) * TPC],
                  label=f"dtl_{idx}", sem=f"qdw{idx % 2}")
            w.wait("vector", f"uwl_{idx}")
            w.wait("vector", f"dtl_{idx}")
            vop("tensor_mul", chunk(dtu_w, hb, TPC),
                chunk(dt_sc, idx % 2, TPC),
                chunk(uw, idx % 2, TPC))
            if wi >= 1:
                w.wait("vector", f"yout_{(wi-1)*HB+hb}")
            else:
                w.wait("vector", f"dtbo_{NC*HB-2}")
                w.wait("vector", f"dtbo_{NC*HB-1}")
            vop("memset",
                y_w[:, hb * TPC:(hb + 1) * TPC],
                0.0, label=f"dtu_{idx}")
            yield
        for n in range(NST):
            bi = wi * NST + n
            w.wait("scalar", f"bcb_{NC-2}")
            w.wait("scalar", f"bcb_{NC-1}")
            if bi >= 2:
                w.wait("scalar", f"bu_last_{bi-2}")
            w.dma("scalar", chunk(Bb, bi % 2, TPC),
                  bounce_bc[n:n + 1,
                            wi * TPC:(wi + 1) * TPC].broadcast_to([128, TPC]),
                  label=f"Bb_{bi}", sem=f"qBb{bi % 2}")
            if bi >= 2:
                w.wait("scalar", f"y_last_{bi-2}")
            w.dma("scalar", chunk(Cb, bi % 2, TPC),
                  bounce_bc[NST + n:NST + n + 1,
                            wi * TPC:(wi + 1) * TPC].broadcast_to([128, TPC]),
                  label=f"Cb_{bi}", sem=f"qCb{bi % 2}")
            for hb in range(HB):
                idx = wi * HB + hb
                si = n_scan[0]
                n_scan[0] += 1
                if si >= 2:
                    w.wait("scalar", f"scan_{si-2}")
                elif si == 0:
                    w.wait("scalar", "aneg_done")
                w.wait("scalar", f"dtl_{idx}")
                act(chunk(a_t, si % 2, TPC),
                    chunk(dt_sc, idx % 2, TPC),
                    AF.Exp, scale=aneg[:, hb * NST + n:hb * NST + n + 1],
                    label=(f"a_{si}" if not (n == NST - 1) else f"a_last_{idx}"))
                if n == NST - 1:
                    w.alias(f"a_{si}", f"a_last_{idx}")
                if si == 0:
                    w.wait("vector", "stats_out")
                    if p.dump:
                        w.wait("vector", "dstat")
                w.wait("vector", f"Bb_{bi}")
                w.wait("vector", f"dtu_{idx}")
                vop("tensor_mul", bu_t[:, 0:TPC], chunk(dtu_w, hb, TPC),
                    chunk(Bb, bi % 2, TPC),
                    label=f"bu_last_{bi}" if hb == HB - 1 else None)
                w.wait("vector", f"a_{si}")
                init = zcol[:, 0:1] if first_of_batch else \
                    carry[:, hb * NST + n:hb * NST + n + 1]

                def scn(e, si=si, init=init):
                    return e.tensor_tensor_scan(
                        out=st_t[:, 0:TPC],
                        data0=chunk(a_t, si % 2, TPC), data1=bu_t[:, 0:TPC],
                        initial=init, op0=ALU.mult, op1=ALU.add)
                w.emit("vector", scn, inc="v", label=f"scan_{si}")
                w.wait("vector", f"scan_{si}")
                vop("tensor_copy",
                    out=carry[:, hb * NST + n:hb * NST + n + 1],
                    in_=st_t[:, 0:TPC][:, TPC - 1:TPC])
                w.wait("vector", f"Cb_{bi}")
                vop("tensor_mul", yt_t[:, 0:TPC], st_t[:, 0:TPC],
                    chunk(Cb, bi % 2, TPC))
                vop("tensor_add",
                    y_w[:, hb * TPC:(hb + 1) * TPC],
                    y_w[:, hb * TPC:(hb + 1) * TPC],
                    yt_t[:, 0:TPC],
                    label=f"y_last_{bi}" if hb == HB - 1 else None)
                yield
        for hb in range(HB):
            idx = wi * HB + hb
            ysl = y_w[:, hb * TPC:(hb + 1) * TPC]
            vop("scalar_tensor_tensor", ysl, chunk(uw, idx % 2, TPC),
                dcol[:, hb:hb + 1], ysl, ALU.mult, ALU.add,
                label=f"ywf_{idx}")
            w.wait("scalar", f"ywf_{idx}")
            w.dma("scalar", aa_in[wi, hb * 128:(hb + 1) * 128, :], ysl,
                  label=f"yout_{idx}", sem="qaa")
            if p.dump:
                w.dma("scalar",
                      dumps["dmp_ymix"][hb * 128:(hb + 1) * 128,
                                        wi * TPC:(wi + 1) * TPC], ysl,
                      label=f"dym_{idx}", sem="qdmp")
                w.wait("vector", f"dym_{idx}")
            yield

    slice_ctr = [0]
    up_hist = {0: None, 1: None, 2: "var_done"}
    up_step = [0]

    def _emit_wslice(inap, width, tag):
        gi = slice_ctr[0]
        slice_ctr[0] += 1
        if gi >= 4:
            w.wait("gpsimd", f"wused_{gi-4}")
        w.dma("gpsimd", chunk(wbf, gi % 4, 2048)[:, 0:width], inap,
              label=f"wld_{gi}", sem=f"qw{gi % 4}")
        return gi

    def emit_ffn_expert(e_):
        nmj = MJS if e_ == NE else MJ
        shared = e_ == NE
        if not shared:
            w.wait("gpsimd", "g_out")
            if e_ >= 1:
                w.wait("gpsimd", f"gbc_used_{e_-1}")
            w.dma("gpsimd", g_bc[:, 0:TPC],
                  bounce_g[e_:e_ + 1, :].broadcast_to([128, TPC]),
                  label=f"gld_{e_}", sem="qgl")
        halves = range(2) if shared else range(1)
        for half in halves:
            mjs = range(half * MJ, half * MJ + MJ) if shared else range(MJ)
            for mj in mjs:
                mloc = mj - (half * MJ if shared else 0)
                if shared:
                    gin = sg_d[mj]
                    uin = su_d[mj]
                else:
                    gin = wg_d[e_, mj]
                    uin = wu_d[e_, mj]
                gi_g = _emit_wslice(gin, DC * 128, f"g{e_}_{mj}")
                gi_u = _emit_wslice(uin, DC * 128, f"u{e_}_{mj}")
                for tt in range(NTT):
                    uidx = f"{e_}_{tt}_{mj}"
                    pair = up_step[0] % 3
                    up_step[0] += 1
                    pg = ps[pair * 2]
                    pu = ps[pair * 2 + 1]
                    tcs = slice(tt * TT, (tt + 1) * TT)
                    if tt == 0:
                        w.wait("tensor", f"wld_{gi_g}")
                        w.wait("tensor", f"wld_{gi_u}")
                    if up_hist[pair] is not None:
                        w.wait("tensor", up_hist[pair])
                    up_hist[pair] = f"hq_{uidx}"
                    for c in range(DC):
                        mm(pg[:, 0:TT],
                           chunk(wbf, gi_g % 4, 2048)[:, c * 128:(c + 1) * 128],
                           chunk(xn, c, TPC)[:, tcs],
                           start=(c == 0), stop=(c == DC - 1),
                           label=f"hg_{uidx}" if c == DC - 1 else None)
                    for c in range(DC):
                        mm(pu[:, 0:TT],
                           chunk(wbf, gi_u % 4, 2048)[:, c * 128:(c + 1) * 128],
                           chunk(xn, c, TPC)[:, tcs],
                           start=(c == 0), stop=(c == DC - 1),
                           label=f"hu_{uidx}" if c == DC - 1 else None)
                    if tt == NTT - 1:
                        w.alias_late(f"wused_{gi_g}", f"hg_{uidx}")
                        w.alias_late(f"wused_{gi_u}", f"hu_{uidx}")
                    w.wait("scalar", f"hg_{uidx}")
                    sqi = (2 * mloc + tt) % 2
                    step = 2 * mloc + tt
                    if step >= 2:
                        pmj, ptt = divmod(step - 2, 2)
                        pmjg = pmj + (half * MJ if shared else 0)
                        w.wait("scalar", f"hq_{e_}_{ptt}_{pmjg}")
                    act(chunk(s_t, sqi, TT), pg[:, 0:TT], AF.Tanh, scale=ccst[:, 2:3])
                    act(chunk(s_t, sqi, TT), chunk(s_t, sqi, TT),
                        AF.Identity, scale=ccst[:, 2:3], bias=ccst[:, 2:3],
                        label=f"sil_{uidx}")
                    w.wait("vector", f"sil_{uidx}")
                    w.wait("vector", f"hu_{uidx}")
                    if mloc == 0 and tt == 0:
                        if shared and half == 1:
                            w.wait("vector", f"dn_last_{e_}_0")
                        elif e_ > 0:
                            w.wait("vector",
                                   f"dn_last_{e_-1}_{1 if e_-1 == NE else 0}")
                    hsl = hprime[:, (mloc * 2 + tt) * TT:(mloc * 2 + tt + 1) * TT]
                    if not shared:
                        if mj == 0 and tt == 0:
                            w.wait("vector", f"gld_{e_}")
                        vop("tensor_mul", chunk(q_t, sqi, TT), pu[:, 0:TT],
                            g_bc[:, tcs])
                        vop("tensor_mul", chunk(q_t, sqi, TT), pg[:, 0:TT],
                            chunk(q_t, sqi, TT))
                        vop("tensor_mul", hsl, chunk(s_t, sqi, TT),
                            chunk(q_t, sqi, TT), label=f"hq_{uidx}")
                    else:
                        vop("tensor_mul", chunk(q_t, sqi, TT), pu[:, 0:TT],
                            chunk(s_t, sqi, TT))
                        vop("tensor_mul", hsl, pg[:, 0:TT],
                            chunk(q_t, sqi, TT), label=f"hq_{uidx}")
                    if tt == NTT - 1:
                        yield
            yield "predown"
            last_mj = half * MJ + MJ - 1 if shared else MJ - 1
            w.wait("tensor", f"hq_{e_}_{NTT-1}_{last_mj}")
            for k in range(DC):
                if shared:
                    inap = sd_d[k][:, half * MJ * 128:(half * MJ + MJ) * 128]
                else:
                    inap = wd_d[e_, k][:, half * MJ * 128:(half * MJ + MJ) * 128]
                gi = _emit_wslice(inap, MJ * 128, f"d{e_}_{k}")
                w.wait("tensor", f"wld_{gi}")
                if e_ == 0 and half == 0 and k < 2:
                    w.wait("tensor", f"dtw_{NC*HB-1}")
                for tt in range(NTT):
                    prev_k = k - (2 if NTT == 1 else 1)
                    if prev_k >= 0:
                        w.wait("tensor", f"ytd_{e_}_{half}_{tt}_{prev_k}")
                    elif e_ > 0 or half == 1:
                        pe, ph = (e_, 0) if half == 1 else \
                            (e_ - 1, 1 if e_ - 1 == NE else 0)
                        pk = DC + prev_k
                        w.wait("tensor", f"ytd_{pe}_{ph}_{tt}_{pk}")
                for j in range(MJ):
                    for tt in range(NTT):
                        pd = ps[6 + tt % 2] if NTT > 1 else ps[6 + k % 2]
                        mm(pd[:, 0:TT],
                           chunk(wbf, gi % 4, 2048)[:, j * 128:(j + 1) * 128],
                           hprime[:, (j * 2 + tt) * TT:(j * 2 + tt + 1) * TT],
                           start=(j == 0), stop=(j == MJ - 1),
                           label=(f"dnm_{e_}_{half}_{tt}_{k}"
                                  if j == MJ - 1 else None))
                for tt in range(NTT):
                    pd = ps[6 + tt % 2] if NTT > 1 else ps[6 + k % 2]
                    tcs = slice(tt * TT, (tt + 1) * TT)
                    kidx = f"{e_}_{half}_{tt}_{k}"
                    w.wait("vector", f"dnm_{kidx}")
                    vop("tensor_add", chunk(y_tot, k, TPC)[:, tcs],
                        chunk(y_tot, k, TPC)[:, tcs], pd[:, 0:TT],
                        label=f"ytd_{kidx}")
                yield
                if True:
                    w.alias_late(f"wused_{gi}", f"dnm_{e_}_{half}_{NTT-1}_{k}")
            last = f"ytd_{e_}_{half}_{NTT-1}_{DC-1}"
            w.alias_late(f"dn_last_{e_}_{half}", last)
        if not shared:
            vop("engine_nop", label=f"gbc_used_{e_}")


    def gchain(gens):
        for g in gens:
            yield from g

    pgen = emit_p34()
    sgen = gchain([emit_scan_window(wi) for wi in range(NC)]) \
        if p.phase in ("all", "scan") else iter(())
    fgen = gchain([emit_ffn_expert(e_) for e_ in range(NE + 1)]) \
        if p.phase in ("all", "ffn") else iter(())
    if p.interleave:
        p_live = True
        f_hold = False
        while p_live:
            try:
                next(pgen)
            except StopIteration:
                p_live = False
            for _ in range(2):
                if f_hold:
                    break
                try:
                    if next(fgen) == "predown":
                        f_hold = True
                except StopIteration:
                    break
        F_TOT = (NE * (MJ + DC) + (MJS + 2 * DC)) or 1
        S_TOT = (NC * (2 + 2 * NST + 2)) or 1
        f_live = s_live = True
        fd = sd = 0
        while f_live or s_live:
            if s_live and (not f_live or fd * S_TOT >= sd * F_TOT):
                try:
                    next(sgen)
                    sd += 1
                except StopIteration:
                    s_live = False
            elif f_live:
                try:
                    next(fgen)
                    fd += 1
                except StopIteration:
                    f_live = False
    else:
        for _ in pgen:
            pass
        for _ in sgen:
            pass
        for _ in fgen:
            pass
    w.resolve_late()

    if p.phase in ("all", "scan"):
        for wi in range(NC):
            for hb in range(HB):
                w.wait("gpsimd", f"yout_{wi * HB + hb}")
        coll("AllToAll", ALU.bypass, [aa_in.opt()], [aa_out.opt()], "cc_aa")

    for c in range(DC):
        if c >= 2:
            w.wait("scalar", f"fin_{c-2}")
        if c == 0 and p.phase in ("all", "scan"):
            w.wait("scalar", f"ywf_{NC*HB-1}")
        w.dma("scalar", chunk(xfin, c % 2, TPC),
              x_own_tok[c * 128:(c + 1) * 128, :], label=f"xf_{c}",
              sem=f"qxf{c % 2}")
        if p.phase in ("all", "scan"):
            w.wait("scalar", "cc_aa")
            w.dma("scalar", chunk(ymr, c % 2, TPC),
                  aa_out[(c * 128) // CH,
                         (c * 128) % CH:(c * 128) % CH + 128, :],
                  label=f"fload_{c}", sem=f"qfl{c % 2}")
            w.wait("vector", f"fload_{c}")
        w.wait("vector", f"xf_{c}")
        if c >= 2:
            w.wait("vector", f"fin_{c-2}")
        if p.phase in ("all", "ffn"):
            w.wait("vector", f"ytd_{NE}_1_{NTT-1}_{c}")
        if p.phase == "all":
            vop("tensor_add", sq, chunk(ymr, c % 2, TPC),
                chunk(y_tot, c, TPC))
        elif p.phase == "scan":
            vop("tensor_copy", out=sq, in_=chunk(ymr, c % 2, TPC))
        elif p.phase == "ffn":
            vop("tensor_copy", out=sq, in_=chunk(y_tot, c, TPC))
        else:
            vop("memset", sq, 0.0)
        vop("tensor_add", chunk(ofin, c % 2, TPC), sq,
            chunk(xfin, c % 2, TPC), label=f"fadd_{c}")
        w.wait("scalar", f"fadd_{c}")
        w.dma("scalar", out_t[c * 128:(c + 1) * 128, :],
              chunk(ofin, c % 2, TPC), label=f"fin_{c}", sem=f"qout{c % 2}")
    w.wait("sync", f"fin_{DC-1}")

    def _final(e):
        for name, sem in w.sems.items():
            if name.startswith("q") and w.count[name] > 0:
                e.wait_ge(sem, w.count[name])
    w.plan["sync"].append(_final)

    with nc.Block() as block:
        @block.sync
        def _(eng):
            for f in w.plan["sync"]:
                f(eng)

        @block.scalar
        def _(eng):
            for f in w.plan["scalar"]:
                f(eng)

        @block.vector
        def _(eng):
            for f in w.plan["vector"]:
                f(eng)

        @block.tensor
        def _(eng):
            for f in w.plan["tensor"]:
                f(eng)

        @block.gpsimd
        def _(eng):
            for f in w.plan["gpsimd"]:
                f(eng)

    return nc


def _prep_inputs(p: P, inputs):
    x = np.ascontiguousarray(inputs["L0_attn_input"], dtype=np.float32)
    S, TPC = p.S, p.TPC
    xT = np.ascontiguousarray(x.reshape(S, D).T)
    wprojT = np.ascontiguousarray(
        np.concatenate([np.asarray(inputs["L0_attn_W_dt_in"]).T,
                        np.asarray(inputs["L0_attn_W_B"]).T,
                        np.asarray(inputs["L0_attn_W_C"]).T],
                       axis=1).astype(np.float32))
    routerT = np.ascontiguousarray(
        np.asarray(inputs["L0_ffn_router_w"]).T.astype(np.float32))
    wdt2T = np.ascontiguousarray(
        np.asarray(inputs["L0_attn_W_dt_out"]).T.astype(np.float32))
    eye128 = np.eye(128, dtype=np.float32)
    def tile_up(wm):
        h = wm.shape[1]
        return np.ascontiguousarray(
            np.asarray(wm, np.float32).reshape(DC, 128, h // 128, 128)
            .transpose(2, 1, 0, 3).reshape(h // 128, 128, DC * 128))

    def tile_dn(wm):
        h = wm.shape[0]
        return np.ascontiguousarray(
            np.asarray(wm, np.float32).reshape(h // 128, 128, DC, 128)
            .transpose(2, 1, 0, 3).reshape(DC, 128, h))

    wg_in = np.asarray(inputs["L0_ffn_wg"], np.float32)
    wu_in = np.asarray(inputs["L0_ffn_wu"], np.float32)
    wd_in = np.asarray(inputs["L0_ffn_wd"], np.float32)
    shared = dict(
        xT=xT, wprojT=wprojT, routerT=routerT, eye128=eye128,
        wg=np.stack([tile_up(wg_in[e]) for e in range(NE)]),
        wu=np.stack([tile_up(wu_in[e]) for e in range(NE)]),
        wd=np.stack([tile_dn(wd_in[e]) for e in range(NE)]),
        sg=tile_up(np.asarray(inputs["L0_ffn_sg"], np.float32)),
        su=tile_up(np.asarray(inputs["L0_ffn_su"], np.float32)),
        sd=tile_dn(np.asarray(inputs["L0_ffn_sd"], np.float32)),
    )
    alog = np.asarray(inputs["L0_attn_A_log"], np.float32)
    dv = np.asarray(inputs["L0_attn_D"], np.float32)
    bdt = np.asarray(inputs["L0_attn_b_dt_out"], np.float32)
    in_maps = []
    for c in range(NC):
        m = dict(shared)
        m["x_own_tok"] = np.ascontiguousarray(xT[:, c * TPC:(c + 1) * TPC])
        m["x_own_ch"] = np.ascontiguousarray(xT[c * CH:(c + 1) * CH, :])
        al = alog[c * CH:(c + 1) * CH].reshape(HB, 128, NST)
        m["alog_own"] = np.ascontiguousarray(
            al.transpose(1, 0, 2).reshape(128, HB * NST))
        m["d_own"] = np.ascontiguousarray(
            dv[c * CH:(c + 1) * CH].reshape(HB, 128).T)
        m["bdt_own"] = np.ascontiguousarray(
            bdt[c * CH:(c + 1) * CH].reshape(HB, 128).T)
        m["wdt2T_own"] = np.ascontiguousarray(wdt2T[:, c * CH:(c + 1) * CH])
        in_maps.append(m)
    return in_maps


def kernel(**inputs):
    p = P()
    in_maps = _prep_inputs(p, inputs)
    nc = build(p)
    res = run_bass_kernel_spmd(nc, in_maps, core_ids=list(range(NC)))
    outT = np.concatenate([res.results[c]["out"] for c in range(NC)], axis=1)
    return np.ascontiguousarray(outT.T).reshape(2, p.S // 2, D)
